# version 43
# baseline (speedup 1.0000x reference)
import math
from bisect import bisect_left
from contextlib import ExitStack

import numpy as np
import ml_dtypes
import concourse.bass as bass
import concourse.mybir as mybir
from concourse.bass_utils import run_bass_kernel_spmd

F32 = mybir.dt.float32
BF16 = mybir.dt.bfloat16
I32 = mybir.dt.int32
AF = mybir.ActivationFunctionType
ALU = mybir.AluOpType

NB = 2
C = 1024
NH = 16
HD = 64
SCALE = 1.0 / math.sqrt(HD)
FF = 4096
EPS = 1e-5
NCORES = 8
ENGINES = ("sync", "scalar", "vector", "tensor", "gpsimd")


class _Rec:
    __slots__ = ("eng", "dma", "dkey", "block", "marked", "waits", "idx", "val")

    def __init__(self, eng, dma, dkey, block, idx):
        self.eng, self.dma, self.dkey = eng, dma, dkey
        self.block, self.idx = block, idx
        self.marked = False
        self.val = None
        self.waits = []


class Prog:

    def __init__(self, nc, mode, rec=None):
        self.nc = nc
        self.mode = mode
        self.block = 0
        if mode == "record":
            self.recs = []
            self.last_writer = {}
            self.readers = {}
            self.marked_by_stream = {}
        else:
            self.recs = rec.recs
            self.cursor = 0
            self.sems = {}
            self._es = ExitStack()
            for e in ENGINES:
                self.sems[("C", e)] = self._es.enter_context(
                    nc.semaphore(f"c_{e}"))
            for k in sorted({r.dkey for r in self.recs if r.dma}):
                self.sems[("D", k)] = self._es.enter_context(
                    nc.semaphore(f"d_{k}"))
            self.pend = {e: [] for e in ENGINES}

    @staticmethod
    def _stream(r):
        return ("D", r.dkey) if r.dma else ("C", r.eng)

    def op(self, eng, fn, reads=(), writes=(), dma=False, dkey=None):
        if self.mode == "record":
            r = _Rec(eng, dma, dkey if dma else None, self.block,
                     len(self.recs))
            deps = {}
            for rr in reads:
                w = self.last_writer.get(rr)
                if w is not None:
                    deps[w.idx] = w
            for wr in writes:
                lw = self.last_writer.get(wr)
                if lw is not None:
                    deps[lw.idx] = lw
                for rd in self.readers.get(wr, ()):
                    deps[rd.idx] = rd
            for d in sorted(deps.values(), key=lambda x: x.idx):
                if d.block != r.block:
                    continue
                if (not d.dma and not dma and d.eng == "tensor"
                        and eng == "tensor"):
                    continue
                st = self._stream(d)
                lst = self.marked_by_stream.setdefault(st, [])
                pos = bisect_left(lst, d.idx)
                tgt = d
                if pos < len(lst):
                    tgt = self.recs[lst[pos]]
                if not tgt.marked:
                    tgt.marked = True
                    tl = self.marked_by_stream.setdefault(self._stream(tgt), [])
                    tl.insert(bisect_left(tl, tgt.idx), tgt.idx)
                r.waits.append(tgt)
            for rr in reads:
                self.readers.setdefault(rr, []).append(r)
            for wr in writes:
                self.last_writer[wr] = r
                self.readers[wr] = []
            self.recs.append(r)
        else:
            r = self.recs[self.cursor]
            self.cursor += 1
            assert r.eng == eng and r.dma == dma, \
                f"pass mismatch at {self.cursor}: {r.eng}/{eng} {r.dma}/{dma}"
            waits = [(self.sems[self._stream(t)], t.val) for t in r.waits]
            inc = None
            step = 0
            if dma:
                inc, step = self.sems[("D", r.dkey)], 16
            elif r.marked:
                inc, step = self.sems[("C", eng)], 1
            self.pend[eng].append((waits, fn, inc, step, r.val))
        return None

    def end_block(self):
        if self.mode == "record":
            self.block += 1
            self.last_writer = {}
            self.readers = {}
        else:
            pend = self.pend
            self.pend = {e: [] for e in ENGINES}
            with self.nc.Block(f"b{self.block}") as blk:
                for eng_name in ENGINES:
                    def body(e, _ops=pend[eng_name]):
                        waited = {}
                        endw = {}
                        for waits, fn, inc, step, val in _ops:
                            for s, v in waits:
                                if waited.get(id(s), -1) < v:
                                    e.wait_ge(s, v)
                                    waited[id(s)] = v
                            instr = fn(e)
                            if inc is not None:
                                instr.then_inc(inc, step)
                                if step == 16:
                                    endw[id(inc)] = (inc, val)
                        for s, v in endw.values():
                            if waited.get(id(s), -1) < v:
                                e.wait_ge(s, v)
                    getattr(blk, eng_name)(body)
            self.block += 1

    def resolve(self):
        ccnt = {e: 0 for e in ENGINES}
        dcnt = {}
        self.block_dma_tot = {}
        for r in self.recs:
            if r.dma:
                dcnt[r.dkey] = dcnt.get(r.dkey, 0) + 1
                r.val = 16 * dcnt[r.dkey]
            elif r.marked:
                ccnt[r.eng] += 1
                r.val = ccnt[r.eng]
            else:
                r.val = None
        return self

    def dma(self, eng, out_ap, in_ap, key, reads=(), writes=(), slow=False):
        def fn(e, o=out_ap, i=in_ap, s=slow):
            return e.dma_start(o, i, allow_slow_non_contiguous=s)
        return self.op(eng, fn, reads, writes, dma=True, dkey=key)

    def mm(self, out_ap, lhsT_ap, rhs_ap, reads, writes, start=True, stop=True,
           skip=False):
        def fn(e, o=out_ap, l=lhsT_ap, r=rhs_ap, st=start, sp=stop, sk=skip):
            return e.matmul(o, l, r, start=st, stop=sp, skip_group_check=sk)
        return self.op("tensor", fn, reads, writes)

    def tr(self, out_ap, in_ap, id_ap, reads, writes):
        def fn(e, o=out_ap, i=in_ap, d=id_ap):
            return e.transpose(o, i, d)
        return self.op("tensor", fn, reads, writes)

    def act(self, out_ap, in_ap, func, reads, writes, bias=None, scale=None,
            accum=None):
        kw = {}
        if bias is not None:
            kw["bias"] = bias
        if scale is not None:
            kw["scale"] = scale
        if accum is not None:
            kw["accum_out"] = accum

        def fn(e, o=out_ap, i=in_ap, f=func, k=kw):
            return e.activation(o, i, f, **k)
        return self.op("scalar", fn, reads, writes)

    def dve(self, name, args, reads, writes, eng="vector", **kw):
        def fn(e, n=name, a=args, k=kw):
            return getattr(e, n)(*a, **k)
        return self.op(eng, fn, reads, writes)


def two_pass(nc, builder):
    p1 = Prog(nc, "record")
    builder(p1, nc)
    p1.resolve()
    p2 = Prog(nc, "emit", rec=p1)
    builder(p2, nc)
    assert p2.cursor == len(p1.recs), (p2.cursor, len(p1.recs))
    return nc


def make_attn_builder(Tb, drams, osh_dst=None):
    NT = NB * Tb
    n_chunks = NT // 512
    nqc = Tb // 512
    ntiles = NT // 128
    (xin, wqin, wkin, wvin, dirsin, hsin, idin, m0in, osh) = drams
    if osh_dst is None:
        osh_dst = lambda hd_, q0: osh[hd_ * 65: hd_ * 65 + 65, q0:q0 + 512]

    def builder(P, nc):
        es = ExitStack()
        with es:
            sb = lambda n, s, d: es.enter_context(nc.sbuf_tensor(n, s, d, allow_name_mangling=True))
            g, sy = "gpsimd", "sync"

            wq_s = sb("wq_s", [128, 8 * 128], BF16)
            wk_s = sb("wk_s", [128, 8 * 128], BF16)
            wv_s = sb("wv_s", [128, 8 * 128], BF16)
            id_bf = sb("id_bf", [128, 128], BF16)
            id_sc = sb("id_sc", [128, 128], BF16)
            m0 = sb("m0s", [128, 128], BF16)
            bpair = sb("bpair", [128, 128], BF16)
            dva = sb("dva", [1, 128], BF16)
            dvb = sb("dvb", [1, 128], BF16)
            dvsa = sb("dvsa", [1, 128], BF16)
            dvsb = sb("dvsb", [1, 128], BF16)
            dva_f = sb("dva_f", [1, 128], F32)
            dvb_f = sb("dvb_f", [1, 128], F32)
            dvs_f = sb("dvs_f", [1, 128], F32)
            hs_s = sb("hs_s", [1, 2], F32)
            onesc = sb("onesc", [128, 1], BF16)
            ones_f = sb("ones_f", [1, 128], F32)
            epst = sb("epst", [128, 1], F32)
            ebias = sb("ebias", [128, 4], F32)
            Kp = sb("Kp", [128, NT], BF16)
            Qp = sb("Qp", [128, NT], BF16)
            Vp = sb("Vp", [128, ntiles * 130], BF16)
            pa_sb = [sb(f"pa_sb{i}", [128, 1024], BF16) for i in range(2)]
            pb_sb = [sb(f"pb_sb{i}", [128, 1024], BF16) for i in range(2)]
            oev = sb("oev", [128, 512], F32)
            obuf = [sb(f"obuf{i}", [128, 512], BF16) for i in range(4)]

            with ExitStack() as es0:
                sb0 = lambda n, s, d: es0.enter_context(nc.sbuf_tensor(n, s, d, allow_name_mangling=True))
                ps0 = lambda n, s, d: es0.enter_context(nc.psum_tensor(n, s, d, allow_name_mangling=True))
                x_t = [sb0(f"x_t{i}", [128, 4 * C], F32) for i in range(2)]
                ln_t = [sb0(f"ln_t{i}", [128, 4 * C], BF16) for i in range(2)]
                ln_f = [sb0(f"ln_f{i}", [128, 8 * 512], BF16) for i in range(2)]
                sq_scr = sb0("sq_scr", [128, C], BF16)
                cp_scr = sb0("cp_scr", [128, C], BF16)
                st_sx = sb0("st_sx", [128, 4], F32)
                st_sq = sb0("st_sq", [128, 4], F32)
                st_mu = sb0("st_mu", [128, 4], F32)
                st_v1 = sb0("st_v1", [128, 4], F32)
                st_v2 = sb0("st_v2", [128, 4], F32)
                st_sd = sb0("st_sd", [128, 4], F32)
                st_r = sb0("st_r", [128, 4], F32)
                qtmp = sb0("qtmp", [128, 512], BF16)
                nsq = sb0("nsq", [128, 2048], BF16)
                nrow = sb0("nrow", [1, 512], F32)
                nmax = sb0("nmax", [1, 64], F32)
                ntot = sb0("ntot", [1, 8], F32)
                p_k = ps0("p_k", [128, 512], F32)
                p_q = ps0("p_q", [128, 512], F32)
                p_qb = ps0("p_qb", [128, 512], F32)
                p_v = ps0("p_v", [128, 128], F32)
                p_t = [ps0(f"p_t{i}", [128, 512], BF16) for i in range(2)]
                p_n = ps0("p_n", [128, 512], F32)

                P.dma(g, id_bf[:], idin, "c_id", writes=["id_bf"])
                P.dma(g, m0[:], m0in, "c_m0", writes=["m0"])
                P.dma(sy, hs_s[:], hsin, "c_hs", writes=["hs_s"])
                P.dma(g, wq_s[:].rearrange("p (cc d) -> p cc d", cc=8),
                      wqin.rearrange("(cc p) d -> p cc d", p=128),
                      "c_wq", writes=["wq_s"])
                P.dma(g, wk_s[:].rearrange("p (cc d) -> p cc d", cc=8),
                      wkin.rearrange("(cc p) d -> p cc d", p=128),
                      "c_wk", writes=["wk_s"])
                P.dma(g, wv_s[:].rearrange("p (cc d) -> p cc d", cc=8),
                      wvin.rearrange("(cc p) d -> p cc d", p=128),
                      "c_wv", writes=["wv_s"])
                P.dve("memset", (dva_f[:], 0.0), [], ["dva_f"])
                P.dve("memset", (dvb_f[:], 0.0), [], ["dvb_f"])
                P.dve("memset", (onesc[:], 1.0), [], ["onesc"])
                P.dve("memset", (ones_f[:], 1.0), [], ["ones_f"])
                P.dve("memset", (epst[:], EPS), [], ["epst"])
                P.dma(g, dva_f[0:1, 0:8], dirsin[0:1, :], "c_dirs",
                      reads=["dva_f"], writes=["dva_f"])
                P.dma(g, dvb_f[0:1, 64:72], dirsin[1:2, :], "c_dirs2",
                      reads=["dvb_f"], writes=["dvb_f"])
                vview = Vp[:, 0:ntiles * 130].rearrange("p (t c2) -> p t c2",
                                                        c2=130)
                P.dve("memset", (vview[:, :, 64:65], 1.0), [], ["Vp"])
                P.dve("memset", (vview[:, :, 129:130], 1.0), ["Vp"], ["Vp"])
                P.dve("tensor_scalar", (dvs_f[0:1, :], dva_f[0:1, :],
                                        hs_s[0:1, 0:1], None),
                      ["dva_f", "hs_s"], ["dvs_f"], op0=ALU.mult)
                P.dve("tensor_copy", (dvsa[:], dvs_f[:]), ["dvs_f"], ["dvsa"])
                P.dve("tensor_scalar", (dvs_f[0:1, :], dvb_f[0:1, :],
                                        hs_s[0:1, 1:2], None),
                      ["dvb_f", "hs_s"], ["dvs_f"], op0=ALU.mult)
                P.dve("tensor_copy", (dvsb[:], dvs_f[:]), ["dvs_f"], ["dvsb"])
                P.dve("tensor_copy", (dva[:], dva_f[:]), ["dva_f"], ["dva"])
                P.dve("tensor_copy", (dvb[:], dvb_f[:]), ["dvb_f"], ["dvb"])
                P.act(id_sc[:], id_bf[:], AF.Copy, ["id_bf"], ["id_sc"],
                      scale=SCALE)
                P.mm(p_k[0:128, 0:128], dva[0:1, :], dvsa[0:1, :],
                     ["dva", "dvsa"], ["p_k"], start=True, stop=False)
                P.mm(p_k[0:128, 0:128], dvb[0:1, :], dvsb[0:1, :],
                     ["dvb", "dvsb"], ["p_k"], start=False, stop=False)
                P.mm(p_k[0:128, 0:128], id_bf[:], id_sc[:], ["id_bf", "id_sc"],
                     ["p_k"], start=False, stop=True)
                P.dve("tensor_copy", (bpair[:], p_k[0:128, 0:128]), ["p_k"],
                      ["bpair"])

                for ch in range(n_chunks):
                    xb, xr = x_t[ch % 2], f"x_t{ch % 2}"
                    lnb, lnr = ln_t[ch % 2], f"ln_t{ch % 2}"
                    lfb, lfr = ln_f[ch % 2], f"ln_f{ch % 2}"
                    P.dma(sy, xb[:].rearrange("p (s c) -> p s c", s=4),
                          xin.rearrange("(ch s p) c -> ch p s c", p=128, s=4)[ch],
                          f"xt{ch % 2}", writes=[xr])
                    for sub in range(4):
                        P.act(sq_scr[:], xb[:, sub * C:(sub + 1) * C], AF.Square,
                              [xr], ["sq_scr", "st_sq"],
                              accum=st_sq[:, sub:sub + 1])
                        P.dve("tensor_scalar",
                              (cp_scr[:], xb[:, sub * C:(sub + 1) * C], 1.0, 0.0),
                              [xr], ["cp_scr", "st_sx"], op0=ALU.mult,
                              op1=ALU.add, accum_out=st_sx[:, sub:sub + 1])
                    P.dve("tensor_scalar", (st_mu[:], st_sx[:], 1.0 / C, None),
                          ["st_sx"], ["st_mu"], op0=ALU.mult)
                    P.dve("tensor_tensor", (st_v1[:], st_mu[:], st_mu[:]),
                          ["st_mu"], ["st_v1"], op=ALU.mult)
                    P.dve("scalar_tensor_tensor",
                          (st_v2[:], st_sq[:], 1.0 / C, st_v1[:]),
                          ["st_sq", "st_v1"], ["st_v2"], op0=ALU.mult,
                          op1=ALU.subtract)
                    P.act(st_sd[:], st_v2[:], AF.Sqrt, ["st_v2", "epst"],
                          ["st_sd"], bias=epst[:, 0:1])
                    P.dve("reciprocal", (st_r[:], st_sd[:]), ["st_sd"], ["st_r"])
                    for sub in range(4):
                        P.dve("tensor_scalar",
                              (lnb[:, sub * C:(sub + 1) * C],
                               xb[:, sub * C:(sub + 1) * C],
                               st_mu[:, sub:sub + 1], st_r[:, sub:sub + 1]),
                              [xr, "st_mu", "st_r"], [lnr],
                              op0=ALU.subtract, op1=ALU.mult)
                    for cc in range(8):
                        pt, ptr_ = p_t[cc % 2], f"p_t{cc % 2}"
                        for sub in range(4):
                            P.tr(pt[:, sub * 128:(sub + 1) * 128],
                                 lnb[:, sub * C + cc * 128: sub * C + cc * 128 + 128],
                                 id_bf[:], [lnr, "id_bf"], [ptr_])
                        if cc % 2 == 0:
                            P.dve("tensor_copy",
                                  (lfb[:, cc * 512:(cc + 1) * 512], pt[:]),
                                  [ptr_], [lfr])
                        else:
                            P.act(lfb[:, cc * 512:(cc + 1) * 512], pt[:],
                                  AF.Copy, [ptr_], [lfr])
                    cols = slice(ch * 512, ch * 512 + 512)
                    for cc in range(8):
                        P.mm(p_k[:], wk_s[:, cc * 128:(cc + 1) * 128],
                             lfb[:, cc * 512:(cc + 1) * 512], ["wk_s", lfr],
                             ["p_k"], start=(cc == 0), stop=(cc == 7))
                    P.act(Kp[:, cols], p_k[:], AF.Copy, ["p_k"], ["Kp"])
                    for cc in range(8):
                        P.mm(p_q[:], wq_s[:, cc * 128:(cc + 1) * 128],
                             lfb[:, cc * 512:(cc + 1) * 512], ["wq_s", lfr],
                             ["p_q"], start=(cc == 0), stop=(cc == 7))
                    P.act(qtmp[:], p_q[:], AF.Copy, ["p_q"], ["qtmp"])
                    P.mm(p_qb[:], bpair[:], qtmp[:], ["bpair", "qtmp"], ["p_qb"])
                    P.act(Qp[:, cols], p_qb[:], AF.Copy, ["p_qb"], ["Qp"])
                    for kt in range(4):
                        gt = ch * 4 + kt
                        for cc in range(8):
                            P.mm(p_v[:],
                                 lfb[:, cc * 512 + kt * 128: cc * 512 + kt * 128 + 128],
                                 wv_s[:, cc * 128:(cc + 1) * 128], [lfr, "wv_s"],
                                 ["p_v"], start=(cc == 0), stop=(cc == 7))
                        P.dve("tensor_copy", (Vp[:, gt * 130: gt * 130 + 64],
                                              p_v[:, 0:64]), ["p_v"], ["Vp"])
                        P.dve("tensor_copy",
                              (Vp[:, gt * 130 + 65: gt * 130 + 129],
                               p_v[:, 64:128]), ["p_v"], ["Vp"])

                seg_len = min(2048, Tb)
                nseg = Tb // seg_len
                nsub = seg_len // 512
                for b in range(NB):
                    for h in range(2):
                        u = b * 2 + h
                        rows = slice(h * 64, h * 64 + 64)
                        for which, buf, br in ((0, Kp, "Kp"), (1, Qp, "Qp")):
                            for seg in range(nseg):
                                s0 = b * Tb + seg * seg_len
                                P.dve("tensor_tensor",
                                      (nsq[rows, 0:seg_len],
                                       buf[rows, s0:s0 + seg_len],
                                       buf[rows, s0:s0 + seg_len]),
                                      [br], ["nsq"], op=ALU.mult)
                                for j in range(nsub):
                                    P.mm(p_n[0:1, :], onesc[rows, 0:1],
                                         nsq[rows, j * 512:(j + 1) * 512],
                                         ["onesc", "nsq"], ["p_n"])
                                    P.act(nrow[:], p_n[0:1, :], AF.Copy,
                                          ["p_n"], ["nrow"])
                                    P.dve("tensor_reduce",
                                          (nmax[0:1, seg * nsub + j: seg * nsub + j + 1],
                                           nrow[:]), ["nrow"], ["nmax"],
                                          axis=mybir.AxisListType.X, op=ALU.max)
                            P.dve("tensor_reduce",
                                  (ntot[0:1, u * 2 + which: u * 2 + which + 1],
                                   nmax[0:1, 0:nseg * nsub]), ["nmax"], ["ntot"],
                                  axis=mybir.AxisListType.X, op=ALU.max)
                        P.dve("tensor_tensor",
                              (nrow[0:1, 0:1], ntot[0:1, u * 2:u * 2 + 1],
                               ntot[0:1, u * 2 + 1:u * 2 + 2]), ["ntot"],
                              ["nrow"], op=ALU.mult)
                        P.act(nrow[0:1, 1:2], nrow[0:1, 0:1], AF.Sqrt,
                              ["nrow"], ["nrow"])
                        P.dve("tensor_scalar", (nrow[0:1, 2:3], nrow[0:1, 1:2],
                                                -1.0, None), ["nrow"], ["nrow"],
                              op0=ALU.mult)
                        P.mm(p_n[:, 0:1], ones_f[:], nrow[0:1, 2:3],
                             ["ones_f", "nrow"], ["p_n"])
                        P.dve("tensor_copy", (ebias[:, u:u + 1], p_n[:, 0:1]),
                              ["p_n"], ["ebias"])
            P.end_block()

            with ExitStack() as es1:
                ps1 = lambda n, s, d: es1.enter_context(nc.psum_tensor(n, s, d, allow_name_mangling=True))
                pa = ps1("pa", [128, 1024], F32)
                pb = ps1("pb", [128, 1024], F32)
                oab = [ps1(f"oa{i}", [128, 512], F32) for i in range(2)]
                obb = [ps1(f"ob{i}", [128, 512], F32) for i in range(2)]
                for b in range(NB):
                    for qc in range(nqc):
                        q0 = b * Tb + qc * 512
                        nkt = 4 * (qc + 1)
                        qsl = slice(q0, q0 + 512)
                        for pr in range(nkt // 2):
                            for half, kt in ((0, 2 * pr), (1, 2 * pr + 1)):
                                ksl = slice(b * Tb + kt * 128,
                                            b * Tb + kt * 128 + 128)
                                hsl = slice(half * 512, half * 512 + 512)
                                P.mm(pa[:, hsl], Kp[0:64, ksl], Qp[0:64, qsl],
                                     ["Kp", "Qp"], ["pa"], skip=True)
                                P.mm(pb[:, hsl], Kp[64:128, ksl],
                                     Qp[64:128, qsl], ["Kp", "Qp"], ["pb"],
                                     skip=True)
                            for hd_, psum, pres in ((0, pa, "pa"), (1, pb, "pb")):
                                psb = (pa_sb, pb_sb)[hd_][pr % 2]
                                pr_ = f"{('pa_sb', 'pb_sb')[hd_]}{pr % 2}"
                                u = b * 2 + hd_
                                bias = ebias[:, u:u + 1]
                                d0 = 2 * pr * 128 - qc * 512
                                d1 = d0 + 128
                                if d1 < 0:
                                    P.act(psb[:], psum[:], AF.Exp,
                                          [pres, "ebias"], [pr_], bias=bias)
                                else:
                                    for half, dd in ((0, d0), (1, d1)):
                                        c0 = half * 512
                                        if dd < 0:
                                            P.act(psb[:, c0:c0 + 512],
                                                  psum[:, c0:c0 + 512], AF.Exp,
                                                  [pres, "ebias"], [pr_],
                                                  bias=bias)
                                        else:
                                            if dd > 0:
                                                P.dve("memset",
                                                      (psb[:, c0:c0 + dd], 0.0),
                                                      [], [pr_])
                                            P.act(psb[:, c0 + dd:c0 + 512],
                                                  psum[:, c0 + dd:c0 + 512],
                                                  AF.Exp, [pres, "ebias"],
                                                  [pr_], bias=bias)
                                            P.dve("tensor_tensor",
                                                  (psb[:, c0 + dd:c0 + dd + 128],
                                                   psb[:, c0 + dd:c0 + dd + 128],
                                                   m0[:]), [pr_, "m0"], [pr_],
                                                  op=ALU.mult)
                            first = pr == 0
                            last = pr == nkt // 2 - 1
                            for half, kt in ((0, 2 * pr), (1, 2 * pr + 1)):
                                gt = (b * Tb) // 128 + kt
                                hsl = slice(half * 512, half * 512 + 512)
                                va = Vp[:, gt * 130: gt * 130 + 65]
                                vb = Vp[:, gt * 130 + 65: gt * 130 + 130]
                                psa = pa_sb[pr % 2]
                                psar = f"pa_sb{pr % 2}"
                                psbb = pb_sb[pr % 2]
                                psbr = f"pb_sb{pr % 2}"
                                st_ = first and half == 0
                                sp_ = last and half == 1
                                P.mm(oab[0][0:65, :], va[0:64, :],
                                     psa[0:64, hsl], ["Vp", psar], ["oa0"],
                                     start=st_, stop=sp_, skip=True)
                                P.mm(oab[1][0:65, :], va[64:128, :],
                                     psa[64:128, hsl], ["Vp", psar], ["oa1"],
                                     start=st_, stop=sp_, skip=True)
                                P.mm(obb[0][0:65, :], vb[0:64, :],
                                     psbb[0:64, hsl], ["Vp", psbr], ["ob0"],
                                     start=st_, stop=sp_, skip=True)
                                P.mm(obb[1][0:65, :], vb[64:128, :],
                                     psbb[64:128, hsl], ["Vp", psbr], ["ob1"],
                                     start=st_, stop=sp_, skip=True)
                        for hd_, opair, onames in ((0, oab, ("oa0", "oa1")),
                                                   (1, obb, ("ob0", "ob1"))):
                            ob_ = obuf[hd_ * 2 + qc % 2]
                            obr = f"obuf{hd_ * 2 + qc % 2}"
                            P.act(oev[0:65, :], opair[0][0:65, :], AF.Copy,
                                  [onames[0]], ["oev"])
                            P.dve("tensor_tensor",
                                  (ob_[0:65, :], oev[0:65, :],
                                   opair[1][0:65, :]),
                                  ["oev", onames[1]], [obr], op=ALU.add)
                            P.dma(sy, osh_dst(hd_, q0),
                                  ob_[0:65, :], obr, reads=[obr, "osh_i"],
                                  writes=["osh_i"])
            P.end_block()

    return builder


def build_attn(Tb):
    NT = NB * Tb
    nc = bass.Bass("TRN2", target_bir_lowering=False)
    drams = (
        nc.dram_tensor("xin", [NT, C], F32, kind="ExternalInput").ap(),
        nc.dram_tensor("wq", [C, 128], F32, kind="ExternalInput").ap(),
        nc.dram_tensor("wk", [C, 128], F32, kind="ExternalInput").ap(),
        nc.dram_tensor("wv", [C, 128], F32, kind="ExternalInput").ap(),
        nc.dram_tensor("dirs", [2, 8], F32, kind="ExternalInput").ap(),
        nc.dram_tensor("hs", [1, 2], F32, kind="ExternalInput").ap(),
        nc.dram_tensor("ident", [128, 128], F32, kind="ExternalInput").ap(),
        nc.dram_tensor("m0", [128, 128], F32, kind="ExternalInput").ap(),
        nc.dram_tensor("osh", [130, NT], BF16, kind="ExternalOutput").ap(),
    )
    return two_pass(nc, make_attn_builder(Tb, drams))


def make_ffn_builder(ntok, drams, og_rows=None, coll=None):
    n_tiles = ntok // 128
    tpc = 2 if n_tiles % 2 == 0 else 1
    W = min(512, ntok)
    (og, xo, woutin, q1in, s1in, b1in, q2in, s2in, b2in, gmin, btin, idin,
     yout, x2d, l2d) = drams
    if og_rows is None:
        og_rows = lambda r, a, b2_: og[r * 130 + a: r * 130 + b2_]

    def builder(P, nc):
        es = ExitStack()
        with es:
            sb = lambda n, s, d: es.enter_context(nc.sbuf_tensor(n, s, d, allow_name_mangling=True))
            g, sy = "gpsimd", "sync"

            w1 = sb("w1", [128, 8 * FF], BF16)
            ones_f = sb("ones_f", [1, 128], F32)
            half_f = sb("half_f", [1, 128], F32)
            epst = sb("epst", [128, 1], F32)
            id_bf = sb("id_bf", [128, 128], BF16)
            grow = sb("grow", [1, C], BF16)
            brow = sb("brow", [1, C], BF16)
            b1row = sb("b1row", [1, FF], BF16)
            b2h = sb("b2h", [1, C], F32)
            gcol = sb("gcol", [128, 8], F32)
            bcol = sb("bcol", [128, 8], F32)
            b1col = sb("b1col", [128, 32], F32)
            b2hbc = sb("b2hbc", [128, C], BF16)
            sq_scr = [sb(f"sq_scr{i}", [128, C], BF16) for i in range(2)]
            st_sx = [sb(f"st_sx{i}", [128, 4], F32) for i in range(4)]
            st_sq = [sb(f"st_sq{i}", [128, 4], F32) for i in range(4)]
            st_mu = [sb(f"st_mu{i}", [128, 4], F32) for i in range(4)]
            st_v1 = [sb(f"st_v1{i}", [128, 4], F32) for i in range(4)]
            st_v2 = [sb(f"st_v2{i}", [128, 4], F32) for i in range(4)]
            st_sd = [sb(f"st_sd{i}", [128, 4], F32) for i in range(4)]
            st_r = [sb(f"st_r{i}", [128, 4], F32) for i in range(4)]
            st_nm = [sb(f"st_nm{i}", [128, 4], F32) for i in range(4)]

            muX = sb("muX", [128, 16], F32)
            rX = sb("rX", [128, 16], F32)
            P.dve("memset", (ones_f[:], 1.0), [], ["ones_f"])
            P.dve("memset", (half_f[:], 0.5), [], ["half_f"])
            P.dve("memset", (epst[:], EPS), [], ["epst"])
            P.dma(g, id_bf[:], idin, "c_id", writes=["id_bf"])
            P.dma(g, grow[:], gmin, "c_gm", writes=["grow"])
            P.dma(g, brow[:], btin, "c_bt", writes=["brow"])
            P.dma(g, b1row[:], b1in, "c_b1", writes=["b1row"])
            P.dma(sy, b2h[:], b2in, "c_b2", writes=["b2h"])

            def stats_ops(si, src_ap, src_res, col):
                sq, sqr = sq_scr[si % 2], f"sq_scr{si % 2}"
                sx, sxr = st_sx[si], f"st_sx{si}"
                sqc, sqcr = st_sq[si], f"st_sq{si}"
                mu, mur = st_mu[si], f"st_mu{si}"
                v1, v1r = st_v1[si], f"st_v1{si}"
                v2, v2r = st_v2[si], f"st_v2{si}"
                sd, sdr = st_sd[si], f"st_sd{si}"
                sr, srr = st_r[si], f"st_r{si}"
                nm, nmr = st_nm[si], f"st_nm{si}"
                P.act(sq[:], src_ap, AF.Square, list(src_res),
                      [sqr, sqcr], accum=sqc[:, col:col + 1])
                P.dve("tensor_reduce", (sx[:, col:col + 1], src_ap),
                      list(src_res), [sxr],
                      axis=mybir.AxisListType.X, op=ALU.add)
                P.dve("tensor_scalar", (mu[:, col:col + 1],
                                        sx[:, col:col + 1], 1.0 / C, None),
                      [sxr], [mur], op0=ALU.mult)
                P.dve("tensor_tensor", (v1[:, col:col + 1],
                                        mu[:, col:col + 1],
                                        mu[:, col:col + 1]),
                      [mur], [v1r], op=ALU.mult)
                P.dve("scalar_tensor_tensor",
                      (v2[:, col:col + 1], sqc[:, col:col + 1], 1.0 / C,
                       v1[:, col:col + 1]), [sqcr, v1r], [v2r],
                      op0=ALU.mult, op1=ALU.subtract)
                P.act(sd[:, col:col + 1], v2[:, col:col + 1], AF.Sqrt,
                      [v2r, "epst"], [sdr], bias=epst[:, 0:1])
                P.dve("reciprocal", (sr[:, col:col + 1],
                                     sd[:, col:col + 1]), [sdr], [srr])
                P.dve("scalar_tensor_tensor",
                      (nm[:, col:col + 1], mu[:, col:col + 1], -1.0,
                       sr[:, col:col + 1]), [mur, srr], [nmr],
                      op0=ALU.mult, op1=ALU.mult)

            with ExitStack() as es0:
                sb0 = lambda n, s, d: es0.enter_context(nc.sbuf_tensor(n, s, d, allow_name_mangling=True))
                ps0 = lambda n, s, d: es0.enter_context(nc.psum_tensor(n, s, d, allow_name_mangling=True))
                wout_s = sb0("wout_s", [128, 8 * C], BF16)
                onorm = sb0("onorm", [128, 8 * ntok], BF16)
                ogl = [sb0(f"ogl{i}", [128, ntok], BF16) for i in range(2)]
                rinvA = [sb0("rinvA0", [1, ntok], F32)] * 2
                rinvB = [sb0("rinvB0", [1, ntok], F32)] * 2
                nscr = sb0("nscr", [128, 512], F32)
                nscr2 = sb0("nscr2", [128, 512], BF16)
                xot = [sb0(f"xot{i}", [128, C], F32) for i in range(2)]
                x2t = [sb0(f"x2t{i}", [128, C], F32) for i in range(2)]
                l2t = [sb0(f"l2t{i}", [128, C], BF16) for i in range(2)]
                stg = [sb0(f"stg{i}", [128, FF // 2], BF16) for i in range(2)]
                srow = [sb0(f"srow{i}", [1, FF // 2], F32) for i in range(2)]
                p_o = [ps0(f"p_o{i}", [128, C], F32) for i in range(2)]
                p_rb = [ps0(f"p_rb{i}", [128, 512], F32) for i in range(2)]
                p_bc = [ps0(f"p_bc{i}", [128, 512], F32) for i in range(2)]

                P.dma(g, wout_s[:].rearrange("p (cc m) -> p cc m", cc=8),
                      woutin.rearrange("(cc p) m -> p cc m", p=128),
                      "c_wout", writes=["wout_s"])
                for r in range(8):
                    ri = r % 2
                    olr = f"ogl{ri}"
                    P.dma(sy, ogl[ri][0:64, :], og_rows(r, 0, 64), olr,
                          reads=["og_i"], writes=[olr])
                    P.dma(sy, ogl[ri][64:128, :], og_rows(r, 65, 129),
                          olr, reads=[olr, "og_i"], writes=[olr])
                    P.dma(g, rinvA[ri][:], og_rows(r, 64, 65), "sbA",
                          reads=["og_i"], writes=["rinvA0"])
                    P.dma(g, rinvB[ri][:], og_rows(r, 129, 130), "sbB",
                          reads=["og_i"], writes=["rinvB0"])
                    for j in range(ntok // W):
                        jc = slice(j * W, j * W + W)
                        pj = (r * (ntok // W) + j) % 2
                        prb, prbr = p_rb[pj], f"p_rb{pj}"
                        P.mm(prb[0:64, 0:W], ones_f[0:1, 0:64],
                             rinvA[ri][0:1, jc],
                             ["ones_f", "rinvA0"], [prbr])
                        P.mm(prb[64:128, 0:W], ones_f[0:1, 0:64],
                             rinvB[ri][0:1, jc], ["ones_f", "rinvB0"],
                             [prbr])
                        P.dve("reciprocal", (nscr[:], prb[:, 0:W]),
                              [prbr], ["nscr"])
                        P.dve("tensor_tensor",
                              (onorm[:, r * ntok + j * W: r * ntok + j * W + W],
                               ogl[ri][:, jc], nscr[:]),
                              [olr, "nscr"], [f"onorm.{r}"], op=ALU.mult)
                for tt in range(n_tiles):
                    ti = tt % 2
                    tsl = slice(tt * 128, tt * 128 + 128)
                    P.dma(sy, xot[ti][:], xo[tsl], f"xot{ti}",
                          writes=[f"xot{ti}"])
                    po, por = p_o[ti], f"p_o{ti}"
                    for mc in range(2):
                        for r in range(8):
                            P.mm(po[:, mc * 512:(mc + 1) * 512],
                                 onorm[:, r * ntok + tt * 128: r * ntok + tt * 128 + 128],
                                 wout_s[:, r * C + mc * 512: r * C + mc * 512 + 512],
                                 [f"onorm.{r}", "wout_s"], [por],
                                 start=(r == 0), stop=(r == 7))
                    stats_ops(ti, po[:], [por], 0)
                    P.dve("tensor_scalar", (xot[ti][:], xot[ti][:],
                                            st_nm[ti][:, 0:1], None),
                          [f"xot{ti}", f"st_nm{ti}"], [f"xot{ti}"],
                          op0=ALU.add)
                    P.dve("scalar_tensor_tensor",
                          (x2t[ti][:], po[:], st_r[ti][:, 0:1], xot[ti][:]),
                          [por, f"st_r{ti}", f"xot{ti}"], [f"x2t{ti}"],
                          op0=ALU.mult, op1=ALU.add)
                    stats_ops(ti, x2t[ti][:], [f"x2t{ti}"], 1)
                    P.dve("tensor_scalar", (l2t[ti][:], x2t[ti][:],
                                            st_mu[ti][:, 1:2],
                                            st_r[ti][:, 1:2]),
                          [f"x2t{ti}", f"st_mu{ti}", f"st_r{ti}"],
                          [f"l2t{ti}"], op0=ALU.subtract, op1=ALU.mult)
                    P.dma("scalar", x2d[tsl], x2t[ti][:], "x2d_w",
                          reads=[f"x2t{ti}"])
                    P.dma("scalar", l2d[tsl], l2t[ti][:], "l2d_w",
                          reads=[f"l2t{ti}"])
                    stats_ops(ti, l2t[ti][:], [f"l2t{ti}"], 2)
                    P.dve("tensor_copy", (muX[:, tt:tt + 1],
                                          st_mu[ti][:, 2:3]),
                          [f"st_mu{ti}"], ["muX"])
                    P.dve("tensor_copy", (rX[:, tt:tt + 1],
                                          st_r[ti][:, 2:3]),
                          [f"st_r{ti}"], ["rX"])
                for cc in range(8):
                    for hh in range(2):
                        P.dma(g, stg[hh][:],
                              q1in.rearrange("(cc p) f -> cc p f", p=128)[cc, :, hh * 2048:(hh + 1) * 2048],
                              f"stg{hh}", writes=[f"stg{hh}"])
                        P.dma(g, srow[hh][0:1, :],
                              s1in[cc:cc + 1, hh * 2048:(hh + 1) * 2048],
                              f"srow{hh}", reads=[f"srow{hh}"],
                              writes=[f"srow{hh}"])
                        for fc in range(4):
                            fs = hh * 2048 + fc * 512
                            pb_, pbr = p_bc[fc % 2], f"p_bc{fc % 2}"
                            P.mm(pb_[:], ones_f[:],
                                 srow[hh][0:1, fc * 512:(fc + 1) * 512],
                                 ["ones_f", f"srow{hh}"], [pbr])
                            P.dve("scalar_tensor_tensor",
                                  (w1[:, cc * FF + fs: cc * FF + fs + 512],
                                   stg[hh][:, fc * 512:(fc + 1) * 512], -128.0,
                                   pb_[:]), [f"stg{hh}", pbr],
                                  [f"w1.{cc}"], op0=ALU.add, op1=ALU.mult)
            P.end_block()

            with ExitStack() as es1:
                sb1 = lambda n, s, d: es1.enter_context(nc.sbuf_tensor(n, s, d, allow_name_mangling=True))
                ps1 = lambda n, s, d: es1.enter_context(nc.psum_tensor(n, s, d, allow_name_mangling=True))
                w2 = sb1("w2", [128, 32 * C], BF16)
                l2c = [sb1(f"l2c{i}", [128, tpc * C], BF16) for i in range(2)]
                x2c = [sb1("x2c0", [128, C], F32)] * 2
                nrmt = sb1("nrmt", [128, C], BF16)
                h_fm = [sb1(f"h_fm{i}", [128, 8 * (tpc * 128)], BF16)
                        for i in range(2)]
                y_fm = sb1("y_fm", [128, 32 * (tpc * 128)], BF16)
                frt = [sb1(f"frt{i}", [128, C], F32) for i in range(2)]
                stg2 = [sb1(f"stg2_{i}", [128, C], BF16) for i in range(2)]
                srow2 = [sb1("srow2_0", [1, C], F32)] * 2
                p_h = ps1("p_h", [128, 1024], BF16)
                p_y = [ps1(f"p_y{i}", [128, tpc * 128], F32) for i in range(2)]
                p_z = [ps1(f"p_z{i}", [128, C], F32) for i in range(2)]
                p_bc2 = ps1("p_bc2", [128, 512], F32)

                TW = tpc * 128
                for cc in range(8):
                    P.tr(p_h[:, 2 * cc:2 * cc + 1],
                         grow[0:1, cc * 128:(cc + 1) * 128],
                         id_bf[0:1, 0:1], ["grow", "id_bf"], ["p_h"])
                for cc in range(8):
                    P.tr(p_h[:, 16 + 2 * cc:17 + 2 * cc],
                         brow[0:1, cc * 128:(cc + 1) * 128], id_bf[0:1, 0:1],
                         ["brow", "id_bf"], ["p_h"])
                for ft in range(32):
                    P.tr(p_h[:, 32 + 2 * ft:33 + 2 * ft],
                         b1row[0:1, ft * 128:(ft + 1) * 128], id_bf[0:1, 0:1],
                         ["b1row", "id_bf"], ["p_h"])
                ph3 = p_h[:, 0:96].rearrange("p (c two) -> p c two", two=2)
                P.dve("tensor_copy", (gcol[:], ph3[:, 0:8, 0:1]), ["p_h"],
                      ["gcol"])
                P.dve("tensor_copy", (bcol[:], ph3[:, 8:16, 0:1]), ["p_h"],
                      ["bcol"])
                P.dve("tensor_copy", (b1col[:], ph3[:, 16:48, 0:1]), ["p_h"],
                      ["b1col"])
                P.dve("tensor_scalar", (srow2[0][:], b2h[:], 0.5, None),
                      ["b2h"], ["srow2_0"], op0=ALU.mult)
                for mc in range(2):
                    P.mm(p_bc2[:], ones_f[:],
                         srow2[0][0:1, mc * 512:(mc + 1) * 512],
                         ["ones_f", "srow2_0"], ["p_bc2"])
                    P.dve("tensor_copy", (b2hbc[:, mc * 512:(mc + 1) * 512],
                                          p_bc2[:]), ["p_bc2"], ["b2hbc"])
                for gch in range(32):
                    gi = gch % 2
                    P.dma(g, stg2[gi][:],
                          q2in.rearrange("(gc p) m -> gc p m", p=128)[gch],
                          f"stg2_{gi}", writes=[f"stg2_{gi}"])
                    P.dma(sy, srow2[gi][:],
                          s2in[gch:gch + 1, :], "srow2_0",
                          reads=["srow2_0"], writes=["srow2_0"])
                    for mc in range(2):
                        P.mm(p_bc2[:], half_f[:],
                             srow2[gi][0:1, mc * 512:(mc + 1) * 512],
                             ["half_f", "srow2_0"], ["p_bc2"])
                        P.dve("scalar_tensor_tensor",
                              (w2[:, gch * C + mc * 512: gch * C + mc * 512 + 512],
                               stg2[gi][:, mc * 512:(mc + 1) * 512], -128.0,
                               p_bc2[:]), [f"stg2_{gi}", "p_bc2"],
                              [f"w2.{gch}"], op0=ALU.add, op1=ALU.mult)
                def load_chn(chn):
                    ci = chn % 2
                    l2b, l2r = l2c[ci], f"l2c{ci}"
                    P.dma(g, l2b[:].rearrange("p (s c2) -> p s c2", s=tpc),
                          l2d.rearrange("(ch s p) c2 -> ch p s c2", p=128,
                                        s=tpc)[chn],
                          f"l2c{ci}", writes=[l2r])

                def prep_chn(chn):
                    ci = chn % 2
                    l2b, l2r = l2c[ci], f"l2c{ci}"
                    hb, hbr = h_fm[ci], f"h_fm{ci}"
                    for tt2 in range(tpc):
                        src = l2b[:, tt2 * C:(tt2 + 1) * C]
                        tt_g = chn * tpc + tt2
                        P.dve("tensor_scalar",
                              (nrmt[:, 0:C], src,
                               muX[:, tt_g:tt_g + 1], rX[:, tt_g:tt_g + 1]),
                              [l2r, "muX", "rX"],
                              ["nrmt"], op0=ALU.subtract, op1=ALU.mult)
                        for cc in range(8):
                            P.tr(p_h[:, cc * 128:(cc + 1) * 128],
                                 nrmt[:, cc * 128: cc * 128 + 128],
                                 id_bf[:], ["nrmt", "id_bf"],
                                 ["p_h"])
                        for cc in range(8):
                            P.dve("tensor_scalar",
                                  (hb[:, cc * TW + tt2 * 128: cc * TW + tt2 * 128 + 128],
                                   p_h[:, cc * 128:(cc + 1) * 128],
                                   gcol[:, cc:cc + 1], bcol[:, cc:cc + 1]),
                                  ["p_h", "gcol", "bcol"],
                                  [f"{hbr}.{cc}.{tt2}"],
                                  op0=ALU.mult, op1=ALU.add)

                nchn = n_tiles // tpc
                load_chn(0)
                prep_chn(0)
                for chn in range(nchn):
                    ci = chn % 2
                    if chn + 1 < nchn:
                        load_chn(chn + 1)
                    l2b, l2r = l2c[ci], f"l2c{ci}"
                    hb, hbr = h_fm[ci], f"h_fm{ci}"
                    for ft in range(32):
                        py, pyr = p_y[ft % 2], f"p_y{ft % 2}"
                        for cc in range(8):
                            P.mm(py[:],
                                 w1[:, cc * FF + ft * 128: cc * FF + ft * 128 + 128],
                                 hb[:, cc * TW:(cc + 1) * TW],
                                 [f"w1.{cc}", f"{hbr}.{cc}.0",
                                  f"{hbr}.{cc}.{tpc - 1}"],
                                 [pyr], start=(cc == 0), stop=(cc == 7))
                        P.act(y_fm[:, ft * TW:(ft + 1) * TW], py[:], AF.Silu,
                              [pyr, "b1col"], [f"y_fm.{ft}"],
                              bias=b1col[:, ft:ft + 1])
                    if chn + 1 < nchn:
                        prep_chn(chn + 1)
                    for tt2 in range(tpc):
                        tt = chn * tpc + tt2
                        tsl = slice(tt * 128, tt * 128 + 128)
                        pz, pzr = p_z[tt2 % 2], f"p_z{tt2 % 2}"
                        fr, frr = frt[tt2], f"frt{tt2}"
                        for mc in range(2):
                            for ft in range(32):
                                P.mm(pz[:, mc * 512:(mc + 1) * 512],
                                     y_fm[:, ft * TW + tt2 * 128: ft * TW + tt2 * 128 + 128],
                                     w2[:, ft * C + mc * 512: ft * C + mc * 512 + 512],
                                     [f"y_fm.{ft}", f"w2.{ft}"], [pzr],
                                     start=(ft == 0), stop=(ft == 31))
                        P.dve("tensor_tensor",
                              (fr[:], pz[:], l2b[:, tt2 * C:(tt2 + 1) * C]),
                              [pzr, l2r], [frr], op=ALU.add)
                        P.dve("tensor_tensor", (fr[:], fr[:], b2hbc[:]),
                              [frr, "b2hbc"], [frr], op=ALU.add)
                        stats_ops(2 + tt2, fr[:], [frr], 1)
                        P.dma("scalar", x2c[tt2][:], x2d[tsl], "x2c0",
                              writes=["x2c0"])
                        P.dve("tensor_scalar", (x2c[tt2][:], x2c[tt2][:],
                                                st_nm[2 + tt2][:, 1:2], None),
                              ["x2c0", f"st_nm{2 + tt2}"], ["x2c0"],
                              op0=ALU.add)
                        P.dve("scalar_tensor_tensor",
                              (fr[:], fr[:], st_r[2 + tt2][:, 1:2],
                               x2c[tt2][:]),
                              [frr, f"st_r{2 + tt2}", "x2c0"], [frr],
                              op0=ALU.mult, op1=ALU.add)
                        P.dma(sy, yout[tsl], fr[:], "y_w", reads=[frr])
            P.end_block()

    return builder




def make_attn_qs_builder(Tb, drams):
    nch = Tb // 512
    nj = nch // 4
    ntok = nj * 512
    ntiles = Tb // 128
    EX = [16 * (j + 1) for j in range(nj)]
    EOFF = [sum(EX[:j]) for j in range(nj)]
    ETOT = sum(EX)
    (xin, wqin, wkin, wvin, dirsin, hsin, idin, m0in, qposin, jmin,
     og_i, k_dram, v_dram, xoin) = drams

    def builder(P, nc):
        es = ExitStack()
        with es:
            sb = lambda n, s, d: es.enter_context(
                nc.sbuf_tensor(n, s, d, allow_name_mangling=True))
            g, sy = "gpsimd", "sync"

            wq_s = sb("wq_s", [128, 8 * 1024], BF16)
            wk_s = sb("wk_s", [128, 8 * 1024], BF16)
            wv_s = sb("wv_s", [128, 8 * 1024], BF16)
            id_bf = sb("id_bf", [128, 128], BF16)
            id_sc = sb("id_sc", [128, 128], BF16)
            m0 = sb("m0s", [128, 128], BF16)
            bpal = sb("bpal", [128, 8 * 128], BF16)
            dva = sb("dva", [1, 128], BF16)
            dvb = sb("dvb", [1, 128], BF16)
            dvsa = sb("dvsa", [1, 128], BF16)
            dvsb = sb("dvsb", [1, 128], BF16)
            dva_f = sb("dva_f", [1, 128], F32)
            dvb_f = sb("dvb_f", [1, 128], F32)
            dvs_f = sb("dvs_f", [1, 128], F32)
            hs_s = sb("hs_s", [1, 16], F32)
            onesc = sb("onesc", [128, 1], BF16)
            ones_f = sb("ones_f", [1, 128], F32)
            epst = sb("epst", [128, 1], F32)
            jm = sb("jm", [128, ETOT], F32)
            qp_s = sb("qp_s", [1, nj], I32)
            Qp = sb("Qp", [128, 8 * ntok], BF16)

            with ExitStack() as es0:
                sb0 = lambda n, s, d: es0.enter_context(
                    nc.sbuf_tensor(n, s, d, allow_name_mangling=True))
                ps0 = lambda n, s, d: es0.enter_context(
                    nc.psum_tensor(n, s, d, allow_name_mangling=True))
                x_t = [sb0(f"x_t{i}", [128, 4 * C], BF16) for i in range(3)]
                ln_t = [sb0(f"ln_t{i}", [128, 4 * C], BF16) for i in range(2)]
                ln_f = [sb0(f"ln_f{i}", [128, 8 * 512], BF16) for i in range(2)]
                sq_scr = [sb0(f"sq_scr{i}", [128, C], BF16) for i in range(2)]
                cp_scr = [sb0(f"cp_scr{i}", [128, C], BF16) for i in range(2)]
                st_sx = [sb0(f"st_sx{i}", [128, 4], F32) for i in range(2)]
                st_sq = [sb0(f"st_sq{i}", [128, 4], F32) for i in range(2)]
                st_mu = [sb0(f"st_mu{i}", [128, 4], F32) for i in range(2)]
                st_v1 = [sb0(f"st_v1{i}", [128, 4], F32) for i in range(2)]
                st_v2 = [sb0(f"st_v2{i}", [128, 4], F32) for i in range(2)]
                st_sd = [sb0(f"st_sd{i}", [128, 4], F32) for i in range(2)]
                st_r = [sb0(f"st_r{i}", [128, 4], F32) for i in range(2)]
                qtmp = [sb0(f"qtmp{i}", [128, 512], BF16)
                        for i in range(2)]
                kstg = [sb0(f"kstg{i}", [128, 512], BF16) for i in range(4)]
                vstg = [sb0(f"vstg{i}", [128, 4 * 8 * 130], BF16)
                        for i in range(3)]
                p_k = [ps0(f"p_k{i}", [128, 512], F32) for i in range(2)]
                p_v = [ps0(f"p_v{i}", [128, 1024], F32) for i in range(2)]
                p_t = [ps0(f"p_t{i}", [128, 512], BF16) for i in range(2)]

                def ln_chunk(ci, xb, xr, lnb, lnr):
                    sqs, sqr = sq_scr[ci], f"sq_scr{ci}"
                    sx, sxr = st_sx[ci], f"st_sx{ci}"
                    sq, sqr2 = st_sq[ci], f"st_sq{ci}"
                    mu, mur = st_mu[ci], f"st_mu{ci}"
                    v1, v1r = st_v1[ci], f"st_v1{ci}"
                    v2, v2r = st_v2[ci], f"st_v2{ci}"
                    sd, sdr = st_sd[ci], f"st_sd{ci}"
                    sr, srr = st_r[ci], f"st_r{ci}"
                    for sub in range(4):
                        P.act(sqs[:], xb[:, sub * C:(sub + 1) * C], AF.Square,
                              [xr], [sqr, sqr2], accum=sq[:, sub:sub + 1])
                    P.dve("tensor_reduce",
                          (sx[:].rearrange("p (s o) -> p s o", o=1),
                           xb[:].rearrange("p (s c) -> p s c", s=4)),
                          [xr], [sxr], axis=mybir.AxisListType.X, op=ALU.add)
                    P.dve("tensor_scalar", (mu[:], sx[:], 1.0 / C, None),
                          [sxr], [mur], op0=ALU.mult)
                    P.dve("tensor_tensor", (v1[:], mu[:], mu[:]), [mur], [v1r],
                          op=ALU.mult)
                    P.dve("scalar_tensor_tensor", (v2[:], sq[:], 1.0 / C,
                                                   v1[:]), [sqr2, v1r], [v2r],
                          op0=ALU.mult, op1=ALU.subtract)
                    P.act(sd[:], v2[:], AF.Sqrt, [v2r, "epst"], [sdr],
                          bias=epst[:, 0:1])
                    P.dve("reciprocal", (sr[:], sd[:]), [sdr], [srr])
                    for sub in range(4):
                        P.dve("tensor_scalar",
                              (lnb[:, sub * C:(sub + 1) * C],
                               xb[:, sub * C:(sub + 1) * C],
                               mu[:, sub:sub + 1], sr[:, sub:sub + 1]),
                              [xr, mur, srr], [lnr],
                              op0=ALU.subtract, op1=ALU.mult)

                P.dma(g, id_bf[:], idin, "c_id", writes=["id_bf"])
                P.dma(g, m0[:], m0in, "c_m0", writes=["m0"])
                P.dma(sy, hs_s[:], hsin, "c_hs", writes=["hs_s"])
                P.dma(sy, jm[:], jmin, "c_jm", writes=["jm"])
                P.dma(sy, qp_s[:], qposin, "c_qp", writes=["qp_s"])
                P.dma(g, wq_s[:].rearrange("p (cc d) -> p cc d", cc=8),
                      wqin.rearrange("(cc p) d -> p cc d", p=128), "c_wq",
                      writes=["wq_s"])
                P.dma(g, wk_s[:].rearrange("p (cc d) -> p cc d", cc=8),
                      wkin.rearrange("(cc p) d -> p cc d", p=128), "c_wk",
                      writes=["wk_s"])
                P.dma(g, wv_s[:].rearrange("p (cc d) -> p cc d", cc=8),
                      wvin.rearrange("(cc p) d -> p cc d", p=128), "c_wv",
                      writes=["wv_s"])
                P.dve("memset", (onesc[:], 1.0), [], ["onesc"])
                P.dve("memset", (ones_f[:], 1.0), [], ["ones_f"])
                P.dve("memset", (epst[:], EPS), [], ["epst"])
                P.act(id_sc[:], id_bf[:], AF.Copy, ["id_bf"], ["id_sc"],
                      scale=SCALE)
                def ln_stage(ch):
                    xb, xr = x_t[ch % 3], f"x_t{ch % 3}"
                    lnb, lnr = ln_t[ch % 2], f"ln_t{ch % 2}"
                    P.dma(g, xb[:].rearrange("p (s c) -> p s c", s=4),
                          xin.rearrange("(ch s p) c -> ch p s c", p=128, s=4)[ch],
                          f"xt{ch % 3}", writes=[xr])
                    ln_chunk(ch % 2, xb, xr, lnb, lnr)

                ln_stage(0)
                for ch in range(nch):
                    lnb, lnr = ln_t[ch % 2], f"ln_t{ch % 2}"
                    lfb, lfr = ln_f[ch % 2], f"ln_f{ch % 2}"
                    if ch + 1 < nch:
                        ln_stage(ch + 1)
                    for cc in range(8):
                        pt, ptr_ = p_t[cc % 2], f"p_t{cc % 2}"
                        for sub in range(4):
                            P.tr(pt[:, sub * 128:(sub + 1) * 128],
                                 lnb[:, sub * C + cc * 128: sub * C + cc * 128 + 128],
                                 id_bf[:], [lnr, "id_bf"], [ptr_])
                        P.dve("tensor_copy",
                              (lfb[:, cc * 512:(cc + 1) * 512], pt[:]),
                              [ptr_], [f"{lfr}.{cc}"])
                    for p in range(8):
                        pk, pkr = p_k[p % 2], f"p_k{p % 2}"
                        for cc in range(8):
                            P.mm(pk[:],
                                 wk_s[:, cc * 1024 + p * 128: cc * 1024 + p * 128 + 128],
                                 lfb[:, cc * 512:(cc + 1) * 512],
                                 ["wk_s", f"{lfr}.{cc}"], [pkr],
                                 start=(cc == 0), stop=(cc == 7))
                        ks, ksr = kstg[p % 4], f"kstg{p % 4}"
                        if p % 2 == 0:
                            P.act(ks[:], pk[:], AF.Copy, [pkr], [ksr])
                        else:
                            P.dve("tensor_copy", (ks[:], pk[:]), [pkr], [ksr])
                        P.dma(sy, k_dram[p][:, ch * 512:(ch + 1) * 512],
                              ks[:], ksr, reads=[ksr], writes=["kspill"])
                    vs, vsr = vstg[ch % 3], f"vstg{ch % 3}"
                    vv = vs[:].rearrange("q (pp kt t) -> q pp kt t", pp=8,
                                         kt=4)
                    P.dve("memset",
                          (vv[:, :, :, 64:130:65].rearrange("q pp kt u -> q pp kt u"), 1.0),
                          [], [vsr])
                    for kt in range(4):
                        pv, pvr = p_v[kt % 2], f"p_v{kt % 2}"
                        for cc in range(8):
                            for mh in range(2):
                                P.mm(pv[:, mh * 512:(mh + 1) * 512],
                                     lfb[:, cc * 512 + kt * 128: cc * 512 + kt * 128 + 128],
                                     wv_s[:, cc * 1024 + mh * 512: cc * 1024 + mh * 512 + 512],
                                     [f"{lfr}.{cc}", "wv_s"], [pvr],
                                     start=(cc == 0), stop=(cc == 7))
                        src = pv[:].rearrange("q (pp two d) -> q pp two d",
                                              pp=8, two=2)
                        dst = vv[:, :, kt, :].rearrange(
                            "q pp (two u) -> q pp two u",
                            two=2)[:, :, :, 0:64]
                        P.dve("tensor_copy", (dst, src), [pvr], [vsr])
                    for p in range(8):
                        P.dma(sy,
                              v_dram[p][:, ch * 520:(ch + 1) * 520],
                              vs[:, p * 520:(p + 1) * 520],
                              vsr, reads=[vsr], writes=["vspill"])

                for p in range(8):
                    P.dve("memset", (dva_f[:], 0.0), [], ["dva_f"])
                    P.dve("memset", (dvb_f[:], 0.0), [], ["dvb_f"])
                    P.dma(g, dva_f[0:1, 0:8], dirsin[2 * p:2 * p + 1, :],
                          "c_dirs", reads=["dva_f"], writes=["dva_f"])
                    P.dma(g, dvb_f[0:1, 64:72], dirsin[2 * p + 1:2 * p + 2, :],
                          "c_dirs2", reads=["dvb_f"], writes=["dvb_f"])
                    P.dve("tensor_scalar", (dvs_f[0:1, :], dva_f[0:1, :],
                                            hs_s[0:1, 2 * p:2 * p + 1], None),
                          ["dva_f", "hs_s"], ["dvs_f"], op0=ALU.mult)
                    P.dve("tensor_copy", (dvsa[:], dvs_f[:]), ["dvs_f"],
                          ["dvsa"])
                    P.dve("tensor_scalar", (dvs_f[0:1, :], dvb_f[0:1, :],
                                            hs_s[0:1, 2 * p + 1:2 * p + 2],
                                            None),
                          ["dvb_f", "hs_s"], ["dvs_f"], op0=ALU.mult)
                    P.dve("tensor_copy", (dvsb[:], dvs_f[:]), ["dvs_f"],
                          ["dvsb"])
                    P.dve("tensor_copy", (dva[:], dva_f[:]), ["dva_f"],
                          ["dva"])
                    P.dve("tensor_copy", (dvb[:], dvb_f[:]), ["dvb_f"],
                          ["dvb"])
                    P.mm(p_k[0][0:128, 0:128], dva[0:1, :], dvsa[0:1, :],
                         ["dva", "dvsa"], ["p_k0"], start=True, stop=False)
                    P.mm(p_k[0][0:128, 0:128], dvb[0:1, :], dvsb[0:1, :],
                         ["dvb", "dvsb"], ["p_k0"], start=False, stop=False)
                    P.mm(p_k[0][0:128, 0:128], id_bf[:], id_sc[:],
                         ["id_bf", "id_sc"], ["p_k0"], start=False, stop=True)
                    P.dve("tensor_copy", (bpal[:, p * 128:(p + 1) * 128],
                                          p_k[0][0:128, 0:128]), ["p_k0"],
                          ["bpal"])

                def qln_stage(u):
                    xb, xr = x_t[u % 3], f"x_t{u % 3}"
                    lnb, lnr = ln_t[u % 2], f"ln_t{u % 2}"
                    P.dma(g, xb[:].rearrange("p (s c) -> p s c", s=4),
                          xoin.rearrange("(u2 s p) c -> u2 p s c", p=128,
                                         s=4)[u],
                          f"xq{u % 3}", writes=[xr])
                    ln_chunk(u % 2, xb, xr, lnb, lnr)

                qln_stage(0)
                for u in range(nj):
                    lnb, lnr = ln_t[u % 2], f"ln_t{u % 2}"
                    lfb, lfr = ln_f[u % 2], f"ln_f{u % 2}"
                    if u + 1 < nj:
                        qln_stage(u + 1)
                    for cc in range(8):
                        pt, ptr_ = p_t[cc % 2], f"p_t{cc % 2}"
                        for sub in range(4):
                            P.tr(pt[:, sub * 128:(sub + 1) * 128],
                                 lnb[:, sub * C + cc * 128: sub * C + cc * 128 + 128],
                                 id_bf[:], [lnr, "id_bf"], [ptr_])
                        P.dve("tensor_copy",
                              (lfb[:, cc * 512:(cc + 1) * 512], pt[:]),
                              [ptr_], [f"{lfr}.{cc}"])
                    for p in range(8):
                        a = p % 2
                        for cc in range(8):
                            P.mm(p_k[a][:],
                                 wq_s[:, cc * 1024 + p * 128: cc * 1024 + p * 128 + 128],
                                 lfb[:, cc * 512:(cc + 1) * 512],
                                 ["wq_s", f"{lfr}.{cc}"], [f"p_k{a}"],
                                 start=(cc == 0), stop=(cc == 7))
                        P.act(qtmp[a][:], p_k[a][:], AF.Copy, [f"p_k{a}"],
                              [f"qtmp{a}"])
                        P.mm(p_v[a][:, 0:512], bpal[:, p * 128:(p + 1) * 128],
                             qtmp[a][:], ["bpal", f"qtmp{a}"], [f"p_v{a}"])
                        P.act(Qp[:, p * ntok + u * 512: p * ntok + u * 512 + 512],
                              p_v[a][:, 0:512], AF.Copy, [f"p_v{a}"], ["Qp"])
            P.end_block()

            with ExitStack() as es1:
                sb1 = lambda n, s, d: es1.enter_context(
                    nc.sbuf_tensor(n, s, d, allow_name_mangling=True))
                ps1 = lambda n, s, d: es1.enter_context(
                    nc.psum_tensor(n, s, d, allow_name_mangling=True))
                Kf = [sb1(f"Kf{i}", [128, Tb], BF16) for i in range(2)]
                Vf = [sb1(f"Vf{i}", [128, ntiles * 130], BF16)
                      for i in range(2)]
                kdg = [sb1(f"kdg{i}", [128, 4 * 512], BF16) for i in range(nj)]
                vdg = [sb1(f"vdg{i}", [128, 4 * 520], BF16) for i in range(nj)]
                pa_sb = [sb1(f"pa_sb{i}", [128, 512], BF16) for i in range(3)]
                pb_sb = [sb1(f"pb_sb{i}", [128, 512], BF16) for i in range(3)]
                oev = sb1("oev", [128, 512], F32)
                obuf = [sb1(f"obuf{i}", [128, 512], BF16) for i in range(4)]
                nsq = sb1("nsq", [128, 4096], BF16)
                nrow = sb1("nrow", [1, 512], F32)
                nmax = sb1("nmax", [1, 64], F32)
                ntot = sb1("ntot", [1, 8], F32)
                ebias = sb1("ebias", [128, 8], F32)
                jb = [sb1(f"jb{i}", [128, ETOT], F32) for i in range(2)]
                pa = [ps1(f"pa{i}", [128, 512], F32) for i in range(2)]
                pb = [ps1(f"pb{i}", [128, 512], F32) for i in range(2)]
                oa = ps1("oa", [128, 512], F32)
                ob = ps1("ob", [128, 512], F32)
                pn = ps1("p_n", [128, 512], F32)

                fcells = {u: [] for u in range(nj)}
                for u in range(nj):
                    def _ldf(e, cell=fcells[u], uu=u):
                        cell.append(e.value_load(qp_s[0:1, uu:uu + 1]))
                        return e.nop()
                    P.op("gpsimd", _ldf, ["qp_s"], [f"fposv{u}"])

                def emit_kv_load(p2):
                    kf_, kfr_ = Kf[p2 % 2], f"Kf{p2 % 2}"
                    vf_, vfr_ = Vf[p2 % 2], f"Vf{p2 % 2}"
                    P.dma(g, kf_[:], k_dram[p2], kfr_, reads=["kspill"],
                          writes=[kfr_])
                    P.dma(g, vf_[:], v_dram[p2], vfr_, reads=["vspill"],
                          writes=[vfr_])

                def norm_steps(p2):
                    kf_, kfr_ = Kf[p2 % 2], f"Kf{p2 % 2}"
                    steps = []
                    for which, buf, br, wid in ((0, kf_, kfr_, Tb),
                                                (1, Qp, "Qp", ntok)):
                        nseg2 = wid // 2048 if wid >= 2048 else 1
                        seglen = min(2048, wid)
                        for seg in range(nseg2):
                            s0 = ((p2 * ntok if which == 1 else 0)
                                  + seg * seglen)
                            sq4 = seg % 2

                            def _sq(buf=buf, br=br, s0=s0, seglen=seglen,
                                    sq4=sq4):
                                P.dve("tensor_tensor",
                                      (nsq[:, sq4 * 2048: sq4 * 2048 + seglen],
                                       buf[:, s0:s0 + seglen],
                                       buf[:, s0:s0 + seglen]), [br],
                                      [f"nsq{sq4}"], op=ALU.mult)
                            steps.append(_sq)
                            for j2 in range(seglen // 512):
                                def _mmr(seg=seg, j2=j2, sq4=sq4):
                                    r2 = 32 * (j2 % 2)
                                    P.mm(pn[r2:r2 + 1, 0:512], onesc[:, 0:1],
                                         nsq[:, sq4 * 2048 + j2 * 512:
                                             sq4 * 2048 + (j2 + 1) * 512],
                                         ["onesc", f"nsq{sq4}"],
                                         [f"p_n.{j2 % 2}"])
                                    P.dve("tensor_reduce",
                                          (nmax[0:1, seg * 4 + j2:
                                                seg * 4 + j2 + 1],
                                           pn[r2:r2 + 1, 0:512]),
                                          [f"p_n.{j2 % 2}"], ["nmax"],
                                          axis=mybir.AxisListType.X,
                                          op=ALU.max)
                                steps.append(_mmr)

                        def _fin(which=which, nseg2=nseg2, seglen=seglen):
                            P.dve("tensor_reduce",
                                  (ntot[0:1, which:which + 1],
                                   nmax[0:1, 0:nseg2 * (seglen // 512)]),
                                  ["nmax"], ["ntot"],
                                  axis=mybir.AxisListType.X, op=ALU.max)
                        steps.append(_fin)

                    def _mul():
                        P.dve("tensor_tensor", (nrow[0:1, 0:1],
                                                ntot[0:1, 0:1],
                                                ntot[0:1, 1:2]), ["ntot"],
                              ["nrow"], op=ALU.mult)
                    steps.append(_mul)
                    return steps

                def emit_norms(p2):
                    for st in norm_steps(p2):
                        st()

                def emit_ebias(p2):
                    jbp, jbr = jb[p2 % 2], f"jb{p2 % 2}"
                    P.act(nrow[0:1, 3:4], nrow[0:1, 0:1], AF.Ln, ["nrow"],
                          ["nrow"])
                    P.act(nrow[0:1, 1:2], nrow[0:1, 3:4], AF.Exp, ["nrow"],
                          ["nrow"], scale=0.5)
                    P.dve("tensor_scalar", (nrow[0:1, 2:3], nrow[0:1, 1:2],
                                            -1.0, None), ["nrow"], ["nrow"],
                          op0=ALU.mult)
                    P.mm(pn[:, 0:1], ones_f[:], nrow[0:1, 2:3],
                         ["ones_f", "nrow"], ["p_n"])
                    P.dve("tensor_copy", (ebias[:, p2:p2 + 1], pn[:, 0:1]),
                          ["p_n"], [f"ebias.{p2}"])
                    P.dve("tensor_scalar", (jbp[:], jm[:],
                                            ebias[:, p2:p2 + 1], None),
                          ["jm", f"ebias.{p2}"], [jbr], op0=ALU.add)

                pend_norm = []
                emit_kv_load(0)
                emit_norms(0)
                emit_ebias(0)
                for p in range(8):
                    if p % 4 == 0:
                        hh = p // 4
                        for u in range(nj):
                            def _dk(e, cell=fcells[u], dst=kdg[u], h2=hh):
                                return e.dma_start(
                                    dst[:].rearrange("q (pp t) -> q pp t",
                                                     pp=4),
                                    k_dram[4 * h2:4 * h2 + 4, :,
                                           bass.ts(cell[0], 512)].rearrange(
                                        "pp q t -> q pp t"))
                            P.op("gpsimd", _dk, [f"fposv{u}", "kspill"],
                                 [f"kdg{u}"], dma=True, dkey=f"kdgd{u}")

                            def _dv(e, cell=fcells[u], dst=vdg[u], h2=hh):
                                return e.dma_start(
                                    dst[:].rearrange("q (pp t) -> q pp t",
                                                     pp=4),
                                    v_dram[4 * h2:4 * h2 + 4, :,
                                           bass.ts(cell[0], 520)].rearrange(
                                        "pp q t -> q pp t"))
                            P.op("gpsimd", _dv, [f"fposv{u}", "vspill"],
                                 [f"vdg{u}"], dma=True, dkey=f"vdgd{u}")
                    kf, kfr = Kf[p % 2], f"Kf{p % 2}"
                    vf, vfr = Vf[p % 2], f"Vf{p % 2}"

                    for u in range(nj):
                        E = EX[u]
                        off = EOFF[u]
                        kd, kdr = kdg[u], f"kdg{u}"
                        vd, vdr = vdg[u], f"vdg{u}"
                        pq = p % 4

                        qsl = slice(p * ntok + u * 512, p * ntok + u * 512 + 512)

                        def emit_s(kt):
                            if kt >= E - 4:
                                d = kt - (E - 4)
                                ka = kd[0:64, pq * 512 + d * 128: pq * 512 + (d + 1) * 128]
                                kb = kd[64:128, pq * 512 + d * 128: pq * 512 + (d + 1) * 128]
                                kres = [kdr, "Qp"]
                            else:
                                ka = kf[0:64, kt * 128:(kt + 1) * 128]
                                kb = kf[64:128, kt * 128:(kt + 1) * 128]
                                kres = [kfr, "Qp"]
                            P.mm(pa[kt % 2][:], ka, Qp[0:64, qsl], kres,
                                 [f"pa{kt % 2}"], skip=True)
                            P.mm(pb[kt % 2][:], kb, Qp[64:128, qsl], kres,
                                 [f"pb{kt % 2}"], skip=True)

                        def emit_exp(kt):
                            diag = kt >= E - 4
                            for hd_, psum, pres in ((0, pa[kt % 2], f"pa{kt % 2}"),
                                                    (1, pb[kt % 2], f"pb{kt % 2}")):
                                psb = (pa_sb, pb_sb)[hd_][kt % 3]
                                pr_ = f"{('pa_sb', 'pb_sb')[hd_]}{kt % 3}"
                                if not diag:
                                    P.act(psb[:], psum[:], AF.Exp,
                                          [pres, f"jb{p % 2}"], [pr_],
                                          bias=jb[p % 2][:, off + kt: off + kt + 1])
                                else:
                                    dd = (kt - (E - 4)) * 128
                                    if dd > 0:
                                        P.dve("memset", (psb[:, 0:dd], 0.0),
                                              [], [pr_])
                                    P.act(psb[:, dd:512], psum[:, dd:512],
                                          AF.Exp, [pres, f"ebias.{p}"], [pr_],
                                          bias=ebias[:, p:p + 1])
                                    P.dve("tensor_tensor",
                                          (psb[:, dd:dd + 128],
                                           psb[:, dd:dd + 128], m0[:]),
                                          [pr_, "m0s"], [pr_], op=ALU.mult)

                        def emit_av(kt):
                            if kt >= E - 4:
                                d = kt - (E - 4)
                                va = vd[:, pq * 520 + d * 130: pq * 520 + d * 130 + 65]
                                vb2 = vd[:, pq * 520 + d * 130 + 65: pq * 520 + d * 130 + 130]
                                vres = vdr
                            else:
                                va = vf[:, kt * 130: kt * 130 + 65]
                                vb2 = vf[:, kt * 130 + 65: kt * 130 + 130]
                                vres = vfr
                            psa = pa_sb[kt % 3]
                            psar = f"pa_sb{kt % 3}"
                            psbb = pb_sb[kt % 3]
                            psbr = f"pb_sb{kt % 3}"
                            st_ = kt == 0
                            sp_ = kt == E - 1
                            P.mm(oa[0:65, :], va, psa[:],
                                 [vres, psar], ["oa"], start=st_,
                                 stop=sp_, skip=True)
                            P.mm(ob[0:65, :], vb2, psbb[:],
                                 [vres, psbr], ["ob"], start=st_,
                                 stop=sp_, skip=True)

                        for kt in range(E + 2):
                            if kt < E:
                                emit_s(kt)
                            if 1 <= kt <= E:
                                emit_exp(kt - 1)
                            if kt >= 2:
                                emit_av(kt - 2)
                            if u == 2 and p < 7 and kt % 2 == 0 and pend_norm:
                                pend_norm.pop(0)()
                        for hd_, opair, oname in ((0, oa, "oa"),
                                                  (1, ob, "ob")):
                            ob_ = obuf[hd_ * 2 + u % 2]
                            obr = f"obuf{hd_ * 2 + u % 2}"
                            P.dve("tensor_copy",
                                  (ob_[0:65, :], opair[0:65, :]),
                                  [oname], [obr])
                            P.dma(sy,
                                  og_i[p, hd_ * 65: hd_ * 65 + 65,
                                       u * 512:(u + 1) * 512],
                                  ob_[0:65, :], obr, reads=[obr, "og_w"],
                                  writes=["og_w"])
                        if u == 0 and p < 7:
                            emit_kv_load(p + 1)
                        if u == 1 and p < 7:
                            pend_norm = norm_steps(p + 1)
                        if u == 2 and p < 7:
                            while pend_norm:
                                pend_norm.pop(0)()
                        if u == 3 and p < 7:
                            emit_ebias(p + 1)
            P.end_block()

    return builder


def build_fused(Tb):
    NT = NB * Tb
    ntok = NT // NCORES
    nj = ntok // 512
    nch = Tb // 512
    ETOT = sum(16 * (j + 1) for j in range(nj))
    ntiles = Tb // 128
    nc = bass.Bass("TRN2", target_bir_lowering=False)
    og_i = nc.dram_tensor("og_i", [NCORES, 130, ntok], BF16).ap()
    k_dram = nc.dram_tensor("k_dram", [8, 128, Tb], BF16).ap()
    v_dram = nc.dram_tensor("v_dram", [8, 128, ntiles * 130], BF16).ap()
    attn_drams = (
        nc.dram_tensor("xin", [Tb, C], F32, kind="ExternalInput").ap(),
        nc.dram_tensor("wq", [C, C], F32, kind="ExternalInput").ap(),
        nc.dram_tensor("wk", [C, C], F32, kind="ExternalInput").ap(),
        nc.dram_tensor("wv", [C, C], F32, kind="ExternalInput").ap(),
        nc.dram_tensor("dirs", [16, 8], F32, kind="ExternalInput").ap(),
        nc.dram_tensor("hs", [1, 16], F32, kind="ExternalInput").ap(),
        nc.dram_tensor("ident", [128, 128], F32, kind="ExternalInput").ap(),
        nc.dram_tensor("m0", [128, 128], F32, kind="ExternalInput").ap(),
        nc.dram_tensor("qpos", [1, nj], I32, kind="ExternalInput").ap(),
        nc.dram_tensor("jm", [128, ETOT], F32, kind="ExternalInput").ap(),
        og_i, k_dram, v_dram, None,
    )
    ffn_drams = (
        None,
        nc.dram_tensor("xo", [ntok, C], F32, kind="ExternalInput").ap(),
        nc.dram_tensor("wout", [C, C], F32, kind="ExternalInput").ap(),
        nc.dram_tensor("q1", [C, FF], I32, kind="ExternalInput").ap(),
        nc.dram_tensor("s1", [8, FF], F32, kind="ExternalInput").ap(),
        nc.dram_tensor("b1", [1, FF], F32, kind="ExternalInput").ap(),
        nc.dram_tensor("q2", [FF, C], I32, kind="ExternalInput").ap(),
        nc.dram_tensor("s2", [32, C], F32, kind="ExternalInput").ap(),
        nc.dram_tensor("b2", [1, C], F32, kind="ExternalInput").ap(),
        nc.dram_tensor("gm", [1, C], F32, kind="ExternalInput").ap(),
        nc.dram_tensor("bt", [1, C], F32, kind="ExternalInput").ap(),
        nc.dram_tensor("ident2", [128, 128], F32, kind="ExternalInput").ap(),
        nc.dram_tensor("y", [ntok, C], F32, kind="ExternalOutput").ap(),
        nc.dram_tensor("x2d", [ntok, C], F32, kind="Internal").ap(),
        nc.dram_tensor("l2d", [ntok, C], BF16, kind="Internal").ap(),
    )

    def og_rows(r, a, b2_):
        return og_i[r, a:b2_, :]

    attn_drams = attn_drams[:-1] + (ffn_drams[1],)
    attn_b = make_attn_qs_builder(Tb, attn_drams)
    ffn_b = make_ffn_builder(ntok, ffn_drams, og_rows=og_rows, coll=None)

    def fused(P, nc_):
        attn_b(P, nc_)
        ffn_b(P, nc_)

    return two_pass(nc, fused)


def core_token_slots(Tb):
    nj = (NB * Tb // NCORES) // 512
    out = {}
    for c in range(4):
        out[c] = [(4 * j + c) if j % 2 == 0 else (4 * j + 3 - c)
                  for j in range(nj)]
    return out


def build_ffn(ntok):
    nc = bass.Bass("TRN2", target_bir_lowering=False)
    drams = (
        nc.dram_tensor("og", [1040, ntok], BF16, kind="ExternalInput").ap(),
        nc.dram_tensor("xo", [ntok, C], F32, kind="ExternalInput").ap(),
        nc.dram_tensor("wout", [C, C], F32, kind="ExternalInput").ap(),
        nc.dram_tensor("q1", [C, FF], I32, kind="ExternalInput").ap(),
        nc.dram_tensor("s1", [8, FF], F32, kind="ExternalInput").ap(),
        nc.dram_tensor("b1", [1, FF], F32, kind="ExternalInput").ap(),
        nc.dram_tensor("q2", [FF, C], I32, kind="ExternalInput").ap(),
        nc.dram_tensor("s2", [32, C], F32, kind="ExternalInput").ap(),
        nc.dram_tensor("b2", [1, C], F32, kind="ExternalInput").ap(),
        nc.dram_tensor("gm", [1, C], F32, kind="ExternalInput").ap(),
        nc.dram_tensor("bt", [1, C], F32, kind="ExternalInput").ap(),
        nc.dram_tensor("ident", [128, 128], F32, kind="ExternalInput").ap(),
        nc.dram_tensor("y", [ntok, C], F32, kind="ExternalOutput").ap(),
        nc.dram_tensor("x2d", [ntok, C], F32, kind="Internal").ap(),
        nc.dram_tensor("l2d", [ntok, C], BF16, kind="Internal").ap(),
    )
    return two_pass(nc, make_ffn_builder(ntok, drams))


T_OVERRIDE = []


def kernel(L0_attn_x, L0_attn_w_qkv, L0_attn_w_out, L0_attn_head_scales,
           L0_attn_head_dirs, L0_ffn_ln_gamma, L0_ffn_ln_beta, L0_ffn_fc1_q,
           L0_ffn_fc1_scales, L0_ffn_fc1_bias, L0_ffn_fc2_q, L0_ffn_fc2_scales,
           L0_ffn_fc2_bias):
    Tb = T_OVERRIDE[0] if T_OVERRIDE else 8192
    NT = NB * Tb
    ntok = NT // NCORES
    nj = ntok // 512
    f32 = np.float32
    x = np.ascontiguousarray(np.asarray(L0_attn_x), dtype=f32).reshape(NT, C)
    wqkv = np.ascontiguousarray(np.asarray(L0_attn_w_qkv), dtype=f32)
    wout = np.ascontiguousarray(np.asarray(L0_attn_w_out), dtype=f32)
    hsc = np.ascontiguousarray(np.asarray(L0_attn_head_scales), dtype=f32)
    hdr = np.ascontiguousarray(np.asarray(L0_attn_head_dirs), dtype=f32)
    gm = np.ascontiguousarray(np.asarray(L0_ffn_ln_gamma), dtype=f32).reshape(1, C)
    bt = np.ascontiguousarray(np.asarray(L0_ffn_ln_beta), dtype=f32).reshape(1, C)
    q1 = np.ascontiguousarray(np.asarray(L0_ffn_fc1_q), dtype=np.int32)
    s1 = np.ascontiguousarray(np.asarray(L0_ffn_fc1_scales), dtype=f32)
    b1 = np.ascontiguousarray(np.asarray(L0_ffn_fc1_bias), dtype=f32).reshape(1, FF)
    q2 = np.ascontiguousarray(np.asarray(L0_ffn_fc2_q), dtype=np.int32)
    s2 = np.ascontiguousarray(np.asarray(L0_ffn_fc2_scales), dtype=f32)
    b2 = np.ascontiguousarray(np.asarray(L0_ffn_fc2_bias), dtype=f32).reshape(1, C)

    ident = np.eye(128, dtype=f32)
    m0 = (np.arange(128)[None, :] >= np.arange(128)[:, None]).astype(f32)
    slots = core_token_slots(Tb)
    EX = [16 * (j + 1) for j in range(nj)]
    ETOT = sum(EX)

    ncf = build_fused(Tb)
    in_maps = []
    for gg in range(NCORES):
        c = gg % 4
        b = gg // 4
        pos = slots[c]
        own_rows = np.concatenate(
            [np.arange(p * 512, p * 512 + 512) + b * Tb for p in pos])
        jm = np.zeros((128, ETOT), dtype=f32)
        off = 0
        for j in range(nj):
            for s in range(EX[j]):
                if s >= 4 * pos[j] and s < EX[j] - 4:
                    jm[:, off + s] = -1e5
            off += EX[j]
        in_maps.append(dict(
            xin=np.ascontiguousarray(x[b * Tb:(b + 1) * Tb]),
            wq=np.ascontiguousarray(wqkv[:, 0:1024]),
            wk=np.ascontiguousarray(wqkv[:, 1024:2048]),
            wv=np.ascontiguousarray(wqkv[:, 2048:3072]),
            dirs=hdr, hs=hsc.reshape(1, 16),
            ident=ident, m0=m0,
            qpos=np.asarray(pos, dtype=np.int32).reshape(1, nj),
            jm=jm,
            xo=np.ascontiguousarray(x[own_rows]),
            wout=wout, q1=q1, s1=s1, b1=b1, q2=q2, s2=s2, b2=b2,
            gm=gm, bt=bt, ident2=ident,
        ))
    res = run_bass_kernel_spmd(ncf, in_maps, core_ids=list(range(NCORES)))
    y = np.empty((NT, C), dtype=np.float32)
    for gg in range(NCORES):
        c = gg % 4
        b = gg // 4
        own_rows = np.concatenate(
            [np.arange(p * 512, p * 512 + 512) + b * Tb for p in slots[c]])
        y[own_rows] = np.asarray(res.results[gg]["y"], dtype=np.float32)
    return y.reshape(NB, Tb, C)


# version 44
# speedup vs baseline: 1.0075x; 1.0075x over previous
import math
from bisect import bisect_left
from contextlib import ExitStack

import numpy as np
import ml_dtypes
import concourse.bass as bass
import concourse.mybir as mybir
from concourse.bass_utils import run_bass_kernel_spmd

F32 = mybir.dt.float32
BF16 = mybir.dt.bfloat16
I32 = mybir.dt.int32
AF = mybir.ActivationFunctionType
ALU = mybir.AluOpType

NB = 2
C = 1024
NH = 16
HD = 64
SCALE = 1.0 / math.sqrt(HD)
FF = 4096
EPS = 1e-5
NCORES = 8
ENGINES = ("sync", "scalar", "vector", "tensor", "gpsimd")


class _Rec:
    __slots__ = ("eng", "dma", "dkey", "block", "marked", "waits", "idx", "val")

    def __init__(self, eng, dma, dkey, block, idx):
        self.eng, self.dma, self.dkey = eng, dma, dkey
        self.block, self.idx = block, idx
        self.marked = False
        self.val = None
        self.waits = []


class Prog:

    def __init__(self, nc, mode, rec=None):
        self.nc = nc
        self.mode = mode
        self.block = 0
        if mode == "record":
            self.recs = []
            self.last_writer = {}
            self.readers = {}
            self.marked_by_stream = {}
        else:
            self.recs = rec.recs
            self.cursor = 0
            self.sems = {}
            self._es = ExitStack()
            for e in ENGINES:
                self.sems[("C", e)] = self._es.enter_context(
                    nc.semaphore(f"c_{e}"))
            for k in sorted({r.dkey for r in self.recs if r.dma}):
                self.sems[("D", k)] = self._es.enter_context(
                    nc.semaphore(f"d_{k}"))
            self.pend = {e: [] for e in ENGINES}

    @staticmethod
    def _stream(r):
        return ("D", r.dkey) if r.dma else ("C", r.eng)

    def op(self, eng, fn, reads=(), writes=(), dma=False, dkey=None):
        if self.mode == "record":
            r = _Rec(eng, dma, dkey if dma else None, self.block,
                     len(self.recs))
            deps = {}
            for rr in reads:
                w = self.last_writer.get(rr)
                if w is not None:
                    deps[w.idx] = w
            for wr in writes:
                lw = self.last_writer.get(wr)
                if lw is not None:
                    deps[lw.idx] = lw
                for rd in self.readers.get(wr, ()):
                    deps[rd.idx] = rd
            for d in sorted(deps.values(), key=lambda x: x.idx):
                if d.block != r.block:
                    continue
                if (not d.dma and not dma and d.eng == "tensor"
                        and eng == "tensor"):
                    continue
                st = self._stream(d)
                lst = self.marked_by_stream.setdefault(st, [])
                pos = bisect_left(lst, d.idx)
                tgt = d
                if pos < len(lst):
                    tgt = self.recs[lst[pos]]
                if not tgt.marked:
                    tgt.marked = True
                    tl = self.marked_by_stream.setdefault(self._stream(tgt), [])
                    tl.insert(bisect_left(tl, tgt.idx), tgt.idx)
                r.waits.append(tgt)
            for rr in reads:
                self.readers.setdefault(rr, []).append(r)
            for wr in writes:
                self.last_writer[wr] = r
                self.readers[wr] = []
            self.recs.append(r)
        else:
            r = self.recs[self.cursor]
            self.cursor += 1
            assert r.eng == eng and r.dma == dma, \
                f"pass mismatch at {self.cursor}: {r.eng}/{eng} {r.dma}/{dma}"
            waits = [(self.sems[self._stream(t)], t.val) for t in r.waits]
            inc = None
            step = 0
            if dma:
                inc, step = self.sems[("D", r.dkey)], 16
            elif r.marked:
                inc, step = self.sems[("C", eng)], 1
            self.pend[eng].append((waits, fn, inc, step, r.val))
        return None

    def end_block(self):
        if self.mode == "record":
            self.block += 1
            self.last_writer = {}
            self.readers = {}
        else:
            pend = self.pend
            self.pend = {e: [] for e in ENGINES}
            with self.nc.Block(f"b{self.block}") as blk:
                for eng_name in ENGINES:
                    def body(e, _ops=pend[eng_name]):
                        waited = {}
                        endw = {}
                        for waits, fn, inc, step, val in _ops:
                            for s, v in waits:
                                if waited.get(id(s), -1) < v:
                                    e.wait_ge(s, v)
                                    waited[id(s)] = v
                            instr = fn(e)
                            if inc is not None:
                                instr.then_inc(inc, step)
                                if step == 16:
                                    endw[id(inc)] = (inc, val)
                        for s, v in endw.values():
                            if waited.get(id(s), -1) < v:
                                e.wait_ge(s, v)
                    getattr(blk, eng_name)(body)
            self.block += 1

    def resolve(self):
        ccnt = {e: 0 for e in ENGINES}
        dcnt = {}
        self.block_dma_tot = {}
        for r in self.recs:
            if r.dma:
                dcnt[r.dkey] = dcnt.get(r.dkey, 0) + 1
                r.val = 16 * dcnt[r.dkey]
            elif r.marked:
                ccnt[r.eng] += 1
                r.val = ccnt[r.eng]
            else:
                r.val = None
        return self

    def dma(self, eng, out_ap, in_ap, key, reads=(), writes=(), slow=False):
        def fn(e, o=out_ap, i=in_ap, s=slow):
            return e.dma_start(o, i, allow_slow_non_contiguous=s)
        return self.op(eng, fn, reads, writes, dma=True, dkey=key)

    def mm(self, out_ap, lhsT_ap, rhs_ap, reads, writes, start=True, stop=True,
           skip=False):
        def fn(e, o=out_ap, l=lhsT_ap, r=rhs_ap, st=start, sp=stop, sk=skip):
            return e.matmul(o, l, r, start=st, stop=sp, skip_group_check=sk)
        return self.op("tensor", fn, reads, writes)

    def tr(self, out_ap, in_ap, id_ap, reads, writes):
        def fn(e, o=out_ap, i=in_ap, d=id_ap):
            return e.transpose(o, i, d)
        return self.op("tensor", fn, reads, writes)

    def act(self, out_ap, in_ap, func, reads, writes, bias=None, scale=None,
            accum=None):
        kw = {}
        if bias is not None:
            kw["bias"] = bias
        if scale is not None:
            kw["scale"] = scale
        if accum is not None:
            kw["accum_out"] = accum

        def fn(e, o=out_ap, i=in_ap, f=func, k=kw):
            return e.activation(o, i, f, **k)
        return self.op("scalar", fn, reads, writes)

    def dve(self, name, args, reads, writes, eng="vector", **kw):
        def fn(e, n=name, a=args, k=kw):
            return getattr(e, n)(*a, **k)
        return self.op(eng, fn, reads, writes)


def two_pass(nc, builder):
    p1 = Prog(nc, "record")
    builder(p1, nc)
    p1.resolve()
    p2 = Prog(nc, "emit", rec=p1)
    builder(p2, nc)
    assert p2.cursor == len(p1.recs), (p2.cursor, len(p1.recs))
    return nc


def make_attn_builder(Tb, drams, osh_dst=None):
    NT = NB * Tb
    n_chunks = NT // 512
    nqc = Tb // 512
    ntiles = NT // 128
    (xin, wqin, wkin, wvin, dirsin, hsin, idin, m0in, osh) = drams
    if osh_dst is None:
        osh_dst = lambda hd_, q0: osh[hd_ * 65: hd_ * 65 + 65, q0:q0 + 512]

    def builder(P, nc):
        es = ExitStack()
        with es:
            sb = lambda n, s, d: es.enter_context(nc.sbuf_tensor(n, s, d, allow_name_mangling=True))
            g, sy = "gpsimd", "sync"

            wq_s = sb("wq_s", [128, 8 * 128], BF16)
            wk_s = sb("wk_s", [128, 8 * 128], BF16)
            wv_s = sb("wv_s", [128, 8 * 128], BF16)
            id_bf = sb("id_bf", [128, 128], BF16)
            id_sc = sb("id_sc", [128, 128], BF16)
            m0 = sb("m0s", [128, 128], BF16)
            bpair = sb("bpair", [128, 128], BF16)
            dva = sb("dva", [1, 128], BF16)
            dvb = sb("dvb", [1, 128], BF16)
            dvsa = sb("dvsa", [1, 128], BF16)
            dvsb = sb("dvsb", [1, 128], BF16)
            dva_f = sb("dva_f", [1, 128], F32)
            dvb_f = sb("dvb_f", [1, 128], F32)
            dvs_f = sb("dvs_f", [1, 128], F32)
            hs_s = sb("hs_s", [1, 2], F32)
            onesc = sb("onesc", [128, 1], BF16)
            ones_f = sb("ones_f", [1, 128], F32)
            epst = sb("epst", [128, 1], F32)
            ebias = sb("ebias", [128, 4], F32)
            Kp = sb("Kp", [128, NT], BF16)
            Qp = sb("Qp", [128, NT], BF16)
            Vp = sb("Vp", [128, ntiles * 130], BF16)
            pa_sb = [sb(f"pa_sb{i}", [128, 1024], BF16) for i in range(2)]
            pb_sb = [sb(f"pb_sb{i}", [128, 1024], BF16) for i in range(2)]
            oev = sb("oev", [128, 512], F32)
            obuf = [sb(f"obuf{i}", [128, 512], BF16) for i in range(4)]

            with ExitStack() as es0:
                sb0 = lambda n, s, d: es0.enter_context(nc.sbuf_tensor(n, s, d, allow_name_mangling=True))
                ps0 = lambda n, s, d: es0.enter_context(nc.psum_tensor(n, s, d, allow_name_mangling=True))
                x_t = [sb0(f"x_t{i}", [128, 4 * C], F32) for i in range(2)]
                ln_t = [sb0(f"ln_t{i}", [128, 4 * C], BF16) for i in range(2)]
                ln_f = [sb0(f"ln_f{i}", [128, 8 * 512], BF16) for i in range(2)]
                sq_scr = sb0("sq_scr", [128, C], BF16)
                cp_scr = sb0("cp_scr", [128, C], BF16)
                st_sx = sb0("st_sx", [128, 4], F32)
                st_sq = sb0("st_sq", [128, 4], F32)
                st_mu = sb0("st_mu", [128, 4], F32)
                st_v1 = sb0("st_v1", [128, 4], F32)
                st_v2 = sb0("st_v2", [128, 4], F32)
                st_sd = sb0("st_sd", [128, 4], F32)
                st_r = sb0("st_r", [128, 4], F32)
                qtmp = sb0("qtmp", [128, 512], BF16)
                nsq = sb0("nsq", [128, 2048], BF16)
                nrow = sb0("nrow", [1, 512], F32)
                nmax = sb0("nmax", [1, 64], F32)
                ntot = sb0("ntot", [1, 8], F32)
                p_k = ps0("p_k", [128, 512], F32)
                p_q = ps0("p_q", [128, 512], F32)
                p_qb = ps0("p_qb", [128, 512], F32)
                p_v = ps0("p_v", [128, 128], F32)
                p_t = [ps0(f"p_t{i}", [128, 512], BF16) for i in range(2)]
                p_n = ps0("p_n", [128, 512], F32)

                P.dma(g, id_bf[:], idin, "c_id", writes=["id_bf"])
                P.dma(g, m0[:], m0in, "c_m0", writes=["m0"])
                P.dma(sy, hs_s[:], hsin, "c_hs", writes=["hs_s"])
                P.dma(g, wq_s[:].rearrange("p (cc d) -> p cc d", cc=8),
                      wqin.rearrange("(cc p) d -> p cc d", p=128),
                      "c_wq", writes=["wq_s"])
                P.dma(g, wk_s[:].rearrange("p (cc d) -> p cc d", cc=8),
                      wkin.rearrange("(cc p) d -> p cc d", p=128),
                      "c_wk", writes=["wk_s"])
                P.dma(g, wv_s[:].rearrange("p (cc d) -> p cc d", cc=8),
                      wvin.rearrange("(cc p) d -> p cc d", p=128),
                      "c_wv", writes=["wv_s"])
                P.dve("memset", (dva_f[:], 0.0), [], ["dva_f"])
                P.dve("memset", (dvb_f[:], 0.0), [], ["dvb_f"])
                P.dve("memset", (onesc[:], 1.0), [], ["onesc"])
                P.dve("memset", (ones_f[:], 1.0), [], ["ones_f"])
                P.dve("memset", (epst[:], EPS), [], ["epst"])
                P.dma(g, dva_f[0:1, 0:8], dirsin[0:1, :], "c_dirs",
                      reads=["dva_f"], writes=["dva_f"])
                P.dma(g, dvb_f[0:1, 64:72], dirsin[1:2, :], "c_dirs2",
                      reads=["dvb_f"], writes=["dvb_f"])
                vview = Vp[:, 0:ntiles * 130].rearrange("p (t c2) -> p t c2",
                                                        c2=130)
                P.dve("memset", (vview[:, :, 64:65], 1.0), [], ["Vp"])
                P.dve("memset", (vview[:, :, 129:130], 1.0), ["Vp"], ["Vp"])
                P.dve("tensor_scalar", (dvs_f[0:1, :], dva_f[0:1, :],
                                        hs_s[0:1, 0:1], None),
                      ["dva_f", "hs_s"], ["dvs_f"], op0=ALU.mult)
                P.dve("tensor_copy", (dvsa[:], dvs_f[:]), ["dvs_f"], ["dvsa"])
                P.dve("tensor_scalar", (dvs_f[0:1, :], dvb_f[0:1, :],
                                        hs_s[0:1, 1:2], None),
                      ["dvb_f", "hs_s"], ["dvs_f"], op0=ALU.mult)
                P.dve("tensor_copy", (dvsb[:], dvs_f[:]), ["dvs_f"], ["dvsb"])
                P.dve("tensor_copy", (dva[:], dva_f[:]), ["dva_f"], ["dva"])
                P.dve("tensor_copy", (dvb[:], dvb_f[:]), ["dvb_f"], ["dvb"])
                P.act(id_sc[:], id_bf[:], AF.Copy, ["id_bf"], ["id_sc"],
                      scale=SCALE)
                P.mm(p_k[0:128, 0:128], dva[0:1, :], dvsa[0:1, :],
                     ["dva", "dvsa"], ["p_k"], start=True, stop=False)
                P.mm(p_k[0:128, 0:128], dvb[0:1, :], dvsb[0:1, :],
                     ["dvb", "dvsb"], ["p_k"], start=False, stop=False)
                P.mm(p_k[0:128, 0:128], id_bf[:], id_sc[:], ["id_bf", "id_sc"],
                     ["p_k"], start=False, stop=True)
                P.dve("tensor_copy", (bpair[:], p_k[0:128, 0:128]), ["p_k"],
                      ["bpair"])

                for ch in range(n_chunks):
                    xb, xr = x_t[ch % 2], f"x_t{ch % 2}"
                    lnb, lnr = ln_t[ch % 2], f"ln_t{ch % 2}"
                    lfb, lfr = ln_f[ch % 2], f"ln_f{ch % 2}"
                    P.dma(sy, xb[:].rearrange("p (s c) -> p s c", s=4),
                          xin.rearrange("(ch s p) c -> ch p s c", p=128, s=4)[ch],
                          f"xt{ch % 2}", writes=[xr])
                    for sub in range(4):
                        P.act(sq_scr[:], xb[:, sub * C:(sub + 1) * C], AF.Square,
                              [xr], ["sq_scr", "st_sq"],
                              accum=st_sq[:, sub:sub + 1])
                        P.dve("tensor_scalar",
                              (cp_scr[:], xb[:, sub * C:(sub + 1) * C], 1.0, 0.0),
                              [xr], ["cp_scr", "st_sx"], op0=ALU.mult,
                              op1=ALU.add, accum_out=st_sx[:, sub:sub + 1])
                    P.dve("tensor_scalar", (st_mu[:], st_sx[:], 1.0 / C, None),
                          ["st_sx"], ["st_mu"], op0=ALU.mult)
                    P.dve("tensor_tensor", (st_v1[:], st_mu[:], st_mu[:]),
                          ["st_mu"], ["st_v1"], op=ALU.mult)
                    P.dve("scalar_tensor_tensor",
                          (st_v2[:], st_sq[:], 1.0 / C, st_v1[:]),
                          ["st_sq", "st_v1"], ["st_v2"], op0=ALU.mult,
                          op1=ALU.subtract)
                    P.act(st_sd[:], st_v2[:], AF.Sqrt, ["st_v2", "epst"],
                          ["st_sd"], bias=epst[:, 0:1])
                    P.dve("reciprocal", (st_r[:], st_sd[:]), ["st_sd"], ["st_r"])
                    for sub in range(4):
                        P.dve("tensor_scalar",
                              (lnb[:, sub * C:(sub + 1) * C],
                               xb[:, sub * C:(sub + 1) * C],
                               st_mu[:, sub:sub + 1], st_r[:, sub:sub + 1]),
                              [xr, "st_mu", "st_r"], [lnr],
                              op0=ALU.subtract, op1=ALU.mult)
                    for cc in range(8):
                        pt, ptr_ = p_t[cc % 2], f"p_t{cc % 2}"
                        for sub in range(4):
                            P.tr(pt[:, sub * 128:(sub + 1) * 128],
                                 lnb[:, sub * C + cc * 128: sub * C + cc * 128 + 128],
                                 id_bf[:], [lnr, "id_bf"], [ptr_])
                        if cc % 2 == 0:
                            P.dve("tensor_copy",
                                  (lfb[:, cc * 512:(cc + 1) * 512], pt[:]),
                                  [ptr_], [lfr])
                        else:
                            P.act(lfb[:, cc * 512:(cc + 1) * 512], pt[:],
                                  AF.Copy, [ptr_], [lfr])
                    cols = slice(ch * 512, ch * 512 + 512)
                    for cc in range(8):
                        P.mm(p_k[:], wk_s[:, cc * 128:(cc + 1) * 128],
                             lfb[:, cc * 512:(cc + 1) * 512], ["wk_s", lfr],
                             ["p_k"], start=(cc == 0), stop=(cc == 7))
                    P.act(Kp[:, cols], p_k[:], AF.Copy, ["p_k"], ["Kp"])
                    for cc in range(8):
                        P.mm(p_q[:], wq_s[:, cc * 128:(cc + 1) * 128],
                             lfb[:, cc * 512:(cc + 1) * 512], ["wq_s", lfr],
                             ["p_q"], start=(cc == 0), stop=(cc == 7))
                    P.act(qtmp[:], p_q[:], AF.Copy, ["p_q"], ["qtmp"])
                    P.mm(p_qb[:], bpair[:], qtmp[:], ["bpair", "qtmp"], ["p_qb"])
                    P.act(Qp[:, cols], p_qb[:], AF.Copy, ["p_qb"], ["Qp"])
                    for kt in range(4):
                        gt = ch * 4 + kt
                        for cc in range(8):
                            P.mm(p_v[:],
                                 lfb[:, cc * 512 + kt * 128: cc * 512 + kt * 128 + 128],
                                 wv_s[:, cc * 128:(cc + 1) * 128], [lfr, "wv_s"],
                                 ["p_v"], start=(cc == 0), stop=(cc == 7))
                        P.dve("tensor_copy", (Vp[:, gt * 130: gt * 130 + 64],
                                              p_v[:, 0:64]), ["p_v"], ["Vp"])
                        P.dve("tensor_copy",
                              (Vp[:, gt * 130 + 65: gt * 130 + 129],
                               p_v[:, 64:128]), ["p_v"], ["Vp"])

                seg_len = min(2048, Tb)
                nseg = Tb // seg_len
                nsub = seg_len // 512
                for b in range(NB):
                    for h in range(2):
                        u = b * 2 + h
                        rows = slice(h * 64, h * 64 + 64)
                        for which, buf, br in ((0, Kp, "Kp"), (1, Qp, "Qp")):
                            for seg in range(nseg):
                                s0 = b * Tb + seg * seg_len
                                P.dve("tensor_tensor",
                                      (nsq[rows, 0:seg_len],
                                       buf[rows, s0:s0 + seg_len],
                                       buf[rows, s0:s0 + seg_len]),
                                      [br], ["nsq"], op=ALU.mult)
                                for j in range(nsub):
                                    P.mm(p_n[0:1, :], onesc[rows, 0:1],
                                         nsq[rows, j * 512:(j + 1) * 512],
                                         ["onesc", "nsq"], ["p_n"])
                                    P.act(nrow[:], p_n[0:1, :], AF.Copy,
                                          ["p_n"], ["nrow"])
                                    P.dve("tensor_reduce",
                                          (nmax[0:1, seg * nsub + j: seg * nsub + j + 1],
                                           nrow[:]), ["nrow"], ["nmax"],
                                          axis=mybir.AxisListType.X, op=ALU.max)
                            P.dve("tensor_reduce",
                                  (ntot[0:1, u * 2 + which: u * 2 + which + 1],
                                   nmax[0:1, 0:nseg * nsub]), ["nmax"], ["ntot"],
                                  axis=mybir.AxisListType.X, op=ALU.max)
                        P.dve("tensor_tensor",
                              (nrow[0:1, 0:1], ntot[0:1, u * 2:u * 2 + 1],
                               ntot[0:1, u * 2 + 1:u * 2 + 2]), ["ntot"],
                              ["nrow"], op=ALU.mult)
                        P.act(nrow[0:1, 1:2], nrow[0:1, 0:1], AF.Sqrt,
                              ["nrow"], ["nrow"])
                        P.dve("tensor_scalar", (nrow[0:1, 2:3], nrow[0:1, 1:2],
                                                -1.0, None), ["nrow"], ["nrow"],
                              op0=ALU.mult)
                        P.mm(p_n[:, 0:1], ones_f[:], nrow[0:1, 2:3],
                             ["ones_f", "nrow"], ["p_n"])
                        P.dve("tensor_copy", (ebias[:, u:u + 1], p_n[:, 0:1]),
                              ["p_n"], ["ebias"])
            P.end_block()

            with ExitStack() as es1:
                ps1 = lambda n, s, d: es1.enter_context(nc.psum_tensor(n, s, d, allow_name_mangling=True))
                pa = ps1("pa", [128, 1024], F32)
                pb = ps1("pb", [128, 1024], F32)
                oab = [ps1(f"oa{i}", [128, 512], F32) for i in range(2)]
                obb = [ps1(f"ob{i}", [128, 512], F32) for i in range(2)]
                for b in range(NB):
                    for qc in range(nqc):
                        q0 = b * Tb + qc * 512
                        nkt = 4 * (qc + 1)
                        qsl = slice(q0, q0 + 512)
                        for pr in range(nkt // 2):
                            for half, kt in ((0, 2 * pr), (1, 2 * pr + 1)):
                                ksl = slice(b * Tb + kt * 128,
                                            b * Tb + kt * 128 + 128)
                                hsl = slice(half * 512, half * 512 + 512)
                                P.mm(pa[:, hsl], Kp[0:64, ksl], Qp[0:64, qsl],
                                     ["Kp", "Qp"], ["pa"], skip=True)
                                P.mm(pb[:, hsl], Kp[64:128, ksl],
                                     Qp[64:128, qsl], ["Kp", "Qp"], ["pb"],
                                     skip=True)
                            for hd_, psum, pres in ((0, pa, "pa"), (1, pb, "pb")):
                                psb = (pa_sb, pb_sb)[hd_][pr % 2]
                                pr_ = f"{('pa_sb', 'pb_sb')[hd_]}{pr % 2}"
                                u = b * 2 + hd_
                                bias = ebias[:, u:u + 1]
                                d0 = 2 * pr * 128 - qc * 512
                                d1 = d0 + 128
                                if d1 < 0:
                                    P.act(psb[:], psum[:], AF.Exp,
                                          [pres, "ebias"], [pr_], bias=bias)
                                else:
                                    for half, dd in ((0, d0), (1, d1)):
                                        c0 = half * 512
                                        if dd < 0:
                                            P.act(psb[:, c0:c0 + 512],
                                                  psum[:, c0:c0 + 512], AF.Exp,
                                                  [pres, "ebias"], [pr_],
                                                  bias=bias)
                                        else:
                                            if dd > 0:
                                                P.dve("memset",
                                                      (psb[:, c0:c0 + dd], 0.0),
                                                      [], [pr_])
                                            P.act(psb[:, c0 + dd:c0 + 512],
                                                  psum[:, c0 + dd:c0 + 512],
                                                  AF.Exp, [pres, "ebias"],
                                                  [pr_], bias=bias)
                                            P.dve("tensor_tensor",
                                                  (psb[:, c0 + dd:c0 + dd + 128],
                                                   psb[:, c0 + dd:c0 + dd + 128],
                                                   m0[:]), [pr_, "m0"], [pr_],
                                                  op=ALU.mult)
                            first = pr == 0
                            last = pr == nkt // 2 - 1
                            for half, kt in ((0, 2 * pr), (1, 2 * pr + 1)):
                                gt = (b * Tb) // 128 + kt
                                hsl = slice(half * 512, half * 512 + 512)
                                va = Vp[:, gt * 130: gt * 130 + 65]
                                vb = Vp[:, gt * 130 + 65: gt * 130 + 130]
                                psa = pa_sb[pr % 2]
                                psar = f"pa_sb{pr % 2}"
                                psbb = pb_sb[pr % 2]
                                psbr = f"pb_sb{pr % 2}"
                                st_ = first and half == 0
                                sp_ = last and half == 1
                                P.mm(oab[0][0:65, :], va[0:64, :],
                                     psa[0:64, hsl], ["Vp", psar], ["oa0"],
                                     start=st_, stop=sp_, skip=True)
                                P.mm(oab[1][0:65, :], va[64:128, :],
                                     psa[64:128, hsl], ["Vp", psar], ["oa1"],
                                     start=st_, stop=sp_, skip=True)
                                P.mm(obb[0][0:65, :], vb[0:64, :],
                                     psbb[0:64, hsl], ["Vp", psbr], ["ob0"],
                                     start=st_, stop=sp_, skip=True)
                                P.mm(obb[1][0:65, :], vb[64:128, :],
                                     psbb[64:128, hsl], ["Vp", psbr], ["ob1"],
                                     start=st_, stop=sp_, skip=True)
                        for hd_, opair, onames in ((0, oab, ("oa0", "oa1")),
                                                   (1, obb, ("ob0", "ob1"))):
                            ob_ = obuf[hd_ * 2 + qc % 2]
                            obr = f"obuf{hd_ * 2 + qc % 2}"
                            P.act(oev[0:65, :], opair[0][0:65, :], AF.Copy,
                                  [onames[0]], ["oev"])
                            P.dve("tensor_tensor",
                                  (ob_[0:65, :], oev[0:65, :],
                                   opair[1][0:65, :]),
                                  ["oev", onames[1]], [obr], op=ALU.add)
                            P.dma(sy, osh_dst(hd_, q0),
                                  ob_[0:65, :], obr, reads=[obr, "osh_i"],
                                  writes=["osh_i"])
            P.end_block()

    return builder


def build_attn(Tb):
    NT = NB * Tb
    nc = bass.Bass("TRN2", target_bir_lowering=False)
    drams = (
        nc.dram_tensor("xin", [NT, C], F32, kind="ExternalInput").ap(),
        nc.dram_tensor("wq", [C, 128], F32, kind="ExternalInput").ap(),
        nc.dram_tensor("wk", [C, 128], F32, kind="ExternalInput").ap(),
        nc.dram_tensor("wv", [C, 128], F32, kind="ExternalInput").ap(),
        nc.dram_tensor("dirs", [2, 8], F32, kind="ExternalInput").ap(),
        nc.dram_tensor("hs", [1, 2], F32, kind="ExternalInput").ap(),
        nc.dram_tensor("ident", [128, 128], F32, kind="ExternalInput").ap(),
        nc.dram_tensor("m0", [128, 128], F32, kind="ExternalInput").ap(),
        nc.dram_tensor("osh", [130, NT], BF16, kind="ExternalOutput").ap(),
    )
    return two_pass(nc, make_attn_builder(Tb, drams))


def make_ffn_builder(ntok, drams, og_rows=None, coll=None):
    n_tiles = ntok // 128
    tpc = 2 if n_tiles % 2 == 0 else 1
    W = min(512, ntok)
    (og, xo, woutin, q1in, s1in, b1in, q2in, s2in, b2in, gmin, btin, idin,
     yout, x2d, l2d) = drams
    if og_rows is None:
        og_rows = lambda r, a, b2_: og[r * 130 + a: r * 130 + b2_]

    def builder(P, nc):
        es = ExitStack()
        with es:
            sb = lambda n, s, d: es.enter_context(nc.sbuf_tensor(n, s, d, allow_name_mangling=True))
            g, sy = "gpsimd", "sync"

            w1 = sb("w1", [128, 8 * FF], BF16)
            ones_f = sb("ones_f", [1, 128], F32)
            half_f = sb("half_f", [1, 128], F32)
            epst = sb("epst", [128, 1], F32)
            id_bf = sb("id_bf", [128, 128], BF16)
            grow = sb("grow", [1, C], BF16)
            brow = sb("brow", [1, C], BF16)
            b1row = sb("b1row", [1, FF], BF16)
            b2h = sb("b2h", [1, C], F32)
            gcol = sb("gcol", [128, 8], F32)
            bcol = sb("bcol", [128, 8], F32)
            b1col = sb("b1col", [128, 32], F32)
            b2hbc = sb("b2hbc", [128, C], BF16)
            sq_scr = [sb(f"sq_scr{i}", [128, C], BF16) for i in range(2)]
            st_sx = [sb(f"st_sx{i}", [128, 4], F32) for i in range(4)]
            st_sq = [sb(f"st_sq{i}", [128, 4], F32) for i in range(4)]
            st_mu = [sb(f"st_mu{i}", [128, 4], F32) for i in range(4)]
            st_v1 = [sb(f"st_v1{i}", [128, 4], F32) for i in range(4)]
            st_v2 = [sb(f"st_v2{i}", [128, 4], F32) for i in range(4)]
            st_sd = [sb(f"st_sd{i}", [128, 4], F32) for i in range(4)]
            st_r = [sb(f"st_r{i}", [128, 4], F32) for i in range(4)]
            st_nm = [sb(f"st_nm{i}", [128, 4], F32) for i in range(4)]

            muX = sb("muX", [128, 16], F32)
            rX = sb("rX", [128, 16], F32)
            P.dve("memset", (ones_f[:], 1.0), [], ["ones_f"])
            P.dve("memset", (half_f[:], 0.5), [], ["half_f"])
            P.dve("memset", (epst[:], EPS), [], ["epst"])
            P.dma(g, id_bf[:], idin, "c_id", writes=["id_bf"])
            P.dma(g, grow[:], gmin, "c_gm", writes=["grow"])
            P.dma(g, brow[:], btin, "c_bt", writes=["brow"])
            P.dma(g, b1row[:], b1in, "c_b1", writes=["b1row"])
            P.dma(sy, b2h[:], b2in, "c_b2", writes=["b2h"])

            def stats_ops(si, src_ap, src_res, col):
                sq, sqr = sq_scr[si % 2], f"sq_scr{si % 2}"
                sx, sxr = st_sx[si], f"st_sx{si}"
                sqc, sqcr = st_sq[si], f"st_sq{si}"
                mu, mur = st_mu[si], f"st_mu{si}"
                v1, v1r = st_v1[si], f"st_v1{si}"
                v2, v2r = st_v2[si], f"st_v2{si}"
                sd, sdr = st_sd[si], f"st_sd{si}"
                sr, srr = st_r[si], f"st_r{si}"
                nm, nmr = st_nm[si], f"st_nm{si}"
                P.act(sq[:], src_ap, AF.Square, list(src_res),
                      [sqr, sqcr], accum=sqc[:, col:col + 1])
                P.dve("tensor_reduce", (sx[:, col:col + 1], src_ap),
                      list(src_res), [sxr],
                      axis=mybir.AxisListType.X, op=ALU.add)
                P.dve("tensor_scalar", (mu[:, col:col + 1],
                                        sx[:, col:col + 1], 1.0 / C, None),
                      [sxr], [mur], op0=ALU.mult)
                P.dve("tensor_tensor", (v1[:, col:col + 1],
                                        mu[:, col:col + 1],
                                        mu[:, col:col + 1]),
                      [mur], [v1r], op=ALU.mult)
                P.dve("scalar_tensor_tensor",
                      (v2[:, col:col + 1], sqc[:, col:col + 1], 1.0 / C,
                       v1[:, col:col + 1]), [sqcr, v1r], [v2r],
                      op0=ALU.mult, op1=ALU.subtract)
                P.act(sd[:, col:col + 1], v2[:, col:col + 1], AF.Sqrt,
                      [v2r, "epst"], [sdr], bias=epst[:, 0:1])
                P.dve("reciprocal", (sr[:, col:col + 1],
                                     sd[:, col:col + 1]), [sdr], [srr])
                P.dve("scalar_tensor_tensor",
                      (nm[:, col:col + 1], mu[:, col:col + 1], -1.0,
                       sr[:, col:col + 1]), [mur, srr], [nmr],
                      op0=ALU.mult, op1=ALU.mult)

            with ExitStack() as es0:
                sb0 = lambda n, s, d: es0.enter_context(nc.sbuf_tensor(n, s, d, allow_name_mangling=True))
                ps0 = lambda n, s, d: es0.enter_context(nc.psum_tensor(n, s, d, allow_name_mangling=True))
                wout_s = sb0("wout_s", [128, 8 * C], BF16)
                onorm = sb0("onorm", [128, 8 * ntok], BF16)
                ogl = [sb0(f"ogl{i}", [128, ntok], BF16) for i in range(2)]
                rinvA = [sb0("rinvA0", [1, ntok], F32)] * 2
                rinvB = [sb0("rinvB0", [1, ntok], F32)] * 2
                nscr = sb0("nscr", [128, 512], F32)
                nscr2 = sb0("nscr2", [128, 512], BF16)
                xot = [sb0(f"xot{i}", [128, C], F32) for i in range(2)]
                x2t = [sb0(f"x2t{i}", [128, C], F32) for i in range(2)]
                l2t = [sb0(f"l2t{i}", [128, C], BF16) for i in range(2)]
                stg = [sb0(f"stg{i}", [128, FF // 2], BF16) for i in range(2)]
                srow = [sb0(f"srow{i}", [1, FF // 2], F32) for i in range(2)]
                p_o = [ps0(f"p_o{i}", [128, C], F32) for i in range(2)]
                p_rb = [ps0(f"p_rb{i}", [128, 512], F32) for i in range(2)]
                p_bc = [ps0(f"p_bc{i}", [128, 512], F32) for i in range(2)]

                P.dma(g, wout_s[:].rearrange("p (cc m) -> p cc m", cc=8),
                      woutin.rearrange("(cc p) m -> p cc m", p=128),
                      "c_wout", writes=["wout_s"])
                for r in range(8):
                    ri = r % 2
                    olr = f"ogl{ri}"
                    P.dma(sy, ogl[ri][0:64, :], og_rows(r, 0, 64), olr,
                          reads=["og_i"], writes=[olr])
                    P.dma(sy, ogl[ri][64:128, :], og_rows(r, 65, 129),
                          olr, reads=[olr, "og_i"], writes=[olr])
                    P.dma(g, rinvA[ri][:], og_rows(r, 64, 65), "sbA",
                          reads=["og_i"], writes=["rinvA0"])
                    P.dma(g, rinvB[ri][:], og_rows(r, 129, 130), "sbB",
                          reads=["og_i"], writes=["rinvB0"])
                    for j in range(ntok // W):
                        jc = slice(j * W, j * W + W)
                        pj = (r * (ntok // W) + j) % 2
                        prb, prbr = p_rb[pj], f"p_rb{pj}"
                        P.mm(prb[0:64, 0:W], ones_f[0:1, 0:64],
                             rinvA[ri][0:1, jc],
                             ["ones_f", "rinvA0"], [prbr])
                        P.mm(prb[64:128, 0:W], ones_f[0:1, 0:64],
                             rinvB[ri][0:1, jc], ["ones_f", "rinvB0"],
                             [prbr])
                        P.dve("reciprocal", (nscr[:], prb[:, 0:W]),
                              [prbr], ["nscr"])
                        P.dve("tensor_tensor",
                              (onorm[:, r * ntok + j * W: r * ntok + j * W + W],
                               ogl[ri][:, jc], nscr[:]),
                              [olr, "nscr"], [f"onorm.{r}"], op=ALU.mult)
                for tt in range(n_tiles):
                    ti = tt % 2
                    tsl = slice(tt * 128, tt * 128 + 128)
                    P.dma(sy, xot[ti][:], xo[tsl], f"xot{ti}",
                          writes=[f"xot{ti}"])
                    po, por = p_o[ti], f"p_o{ti}"
                    for mc in range(2):
                        for r in range(8):
                            P.mm(po[:, mc * 512:(mc + 1) * 512],
                                 onorm[:, r * ntok + tt * 128: r * ntok + tt * 128 + 128],
                                 wout_s[:, r * C + mc * 512: r * C + mc * 512 + 512],
                                 [f"onorm.{r}", "wout_s"], [por],
                                 start=(r == 0), stop=(r == 7))
                    stats_ops(ti, po[:], [por], 0)
                    P.dve("tensor_scalar", (xot[ti][:], xot[ti][:],
                                            st_nm[ti][:, 0:1], None),
                          [f"xot{ti}", f"st_nm{ti}"], [f"xot{ti}"],
                          op0=ALU.add)
                    P.dve("scalar_tensor_tensor",
                          (x2t[ti][:], po[:], st_r[ti][:, 0:1], xot[ti][:]),
                          [por, f"st_r{ti}", f"xot{ti}"], [f"x2t{ti}"],
                          op0=ALU.mult, op1=ALU.add)
                    stats_ops(ti, x2t[ti][:], [f"x2t{ti}"], 1)
                    P.dve("tensor_scalar", (l2t[ti][:], x2t[ti][:],
                                            st_mu[ti][:, 1:2],
                                            st_r[ti][:, 1:2]),
                          [f"x2t{ti}", f"st_mu{ti}", f"st_r{ti}"],
                          [f"l2t{ti}"], op0=ALU.subtract, op1=ALU.mult)
                    P.dma("scalar", x2d[tsl], x2t[ti][:], "x2d_w",
                          reads=[f"x2t{ti}"])
                    P.dma("scalar", l2d[tsl], l2t[ti][:], "l2d_w",
                          reads=[f"l2t{ti}"])
                    stats_ops(ti, l2t[ti][:], [f"l2t{ti}"], 2)
                    P.dve("tensor_copy", (muX[:, tt:tt + 1],
                                          st_mu[ti][:, 2:3]),
                          [f"st_mu{ti}"], ["muX"])
                    P.dve("tensor_copy", (rX[:, tt:tt + 1],
                                          st_r[ti][:, 2:3]),
                          [f"st_r{ti}"], ["rX"])
                for cc in range(8):
                    for hh in range(2):
                        P.dma(g, stg[hh][:],
                              q1in.rearrange("(cc p) f -> cc p f", p=128)[cc, :, hh * 2048:(hh + 1) * 2048],
                              f"stg{hh}", writes=[f"stg{hh}"])
                        P.dma(g, srow[hh][0:1, :],
                              s1in[cc:cc + 1, hh * 2048:(hh + 1) * 2048],
                              f"srow{hh}", reads=[f"srow{hh}"],
                              writes=[f"srow{hh}"])
                        for fc in range(4):
                            fs = hh * 2048 + fc * 512
                            pb_, pbr = p_bc[fc % 2], f"p_bc{fc % 2}"
                            P.mm(pb_[:], ones_f[:],
                                 srow[hh][0:1, fc * 512:(fc + 1) * 512],
                                 ["ones_f", f"srow{hh}"], [pbr])
                            P.dve("scalar_tensor_tensor",
                                  (w1[:, cc * FF + fs: cc * FF + fs + 512],
                                   stg[hh][:, fc * 512:(fc + 1) * 512], -128.0,
                                   pb_[:]), [f"stg{hh}", pbr],
                                  [f"w1.{cc}"], op0=ALU.add, op1=ALU.mult)
            P.end_block()

            with ExitStack() as es1:
                sb1 = lambda n, s, d: es1.enter_context(nc.sbuf_tensor(n, s, d, allow_name_mangling=True))
                ps1 = lambda n, s, d: es1.enter_context(nc.psum_tensor(n, s, d, allow_name_mangling=True))
                w2 = sb1("w2", [128, 32 * C], BF16)
                l2c = [sb1(f"l2c{i}", [128, tpc * C], BF16) for i in range(2)]
                x2c = [sb1("x2c0", [128, C], F32)] * 2
                nrmt = sb1("nrmt", [128, C], BF16)
                h_fm = [sb1(f"h_fm{i}", [128, 8 * (tpc * 128)], BF16)
                        for i in range(2)]
                y_fm = sb1("y_fm", [128, 32 * (tpc * 128)], BF16)
                frt = [sb1(f"frt{i}", [128, C], F32) for i in range(2)]
                stg2 = [sb1(f"stg2_{i}", [128, C], BF16) for i in range(2)]
                srow2 = [sb1("srow2_0", [1, C], F32)] * 2
                p_h = ps1("p_h", [128, 1024], BF16)
                p_y = [ps1(f"p_y{i}", [128, tpc * 128], F32) for i in range(2)]
                p_z = [ps1(f"p_z{i}", [128, C], F32) for i in range(2)]
                p_bc2 = ps1("p_bc2", [128, 512], F32)

                TW = tpc * 128
                for cc in range(8):
                    P.tr(p_h[:, 2 * cc:2 * cc + 1],
                         grow[0:1, cc * 128:(cc + 1) * 128],
                         id_bf[0:1, 0:1], ["grow", "id_bf"], ["p_h"])
                for cc in range(8):
                    P.tr(p_h[:, 16 + 2 * cc:17 + 2 * cc],
                         brow[0:1, cc * 128:(cc + 1) * 128], id_bf[0:1, 0:1],
                         ["brow", "id_bf"], ["p_h"])
                for ft in range(32):
                    P.tr(p_h[:, 32 + 2 * ft:33 + 2 * ft],
                         b1row[0:1, ft * 128:(ft + 1) * 128], id_bf[0:1, 0:1],
                         ["b1row", "id_bf"], ["p_h"])
                ph3 = p_h[:, 0:96].rearrange("p (c two) -> p c two", two=2)
                P.dve("tensor_copy", (gcol[:], ph3[:, 0:8, 0:1]), ["p_h"],
                      ["gcol"])
                P.dve("tensor_copy", (bcol[:], ph3[:, 8:16, 0:1]), ["p_h"],
                      ["bcol"])
                P.dve("tensor_copy", (b1col[:], ph3[:, 16:48, 0:1]), ["p_h"],
                      ["b1col"])
                P.dve("tensor_scalar", (srow2[0][:], b2h[:], 0.5, None),
                      ["b2h"], ["srow2_0"], op0=ALU.mult)
                for mc in range(2):
                    P.mm(p_bc2[:], ones_f[:],
                         srow2[0][0:1, mc * 512:(mc + 1) * 512],
                         ["ones_f", "srow2_0"], ["p_bc2"])
                    P.dve("tensor_copy", (b2hbc[:, mc * 512:(mc + 1) * 512],
                                          p_bc2[:]), ["p_bc2"], ["b2hbc"])
                for gch in range(32):
                    gi = gch % 2
                    P.dma(g, stg2[gi][:],
                          q2in.rearrange("(gc p) m -> gc p m", p=128)[gch],
                          f"stg2_{gi}", writes=[f"stg2_{gi}"])
                    P.dma(sy, srow2[gi][:],
                          s2in[gch:gch + 1, :], "srow2_0",
                          reads=["srow2_0"], writes=["srow2_0"])
                    for mc in range(2):
                        P.mm(p_bc2[:], half_f[:],
                             srow2[gi][0:1, mc * 512:(mc + 1) * 512],
                             ["half_f", "srow2_0"], ["p_bc2"])
                        P.dve("scalar_tensor_tensor",
                              (w2[:, gch * C + mc * 512: gch * C + mc * 512 + 512],
                               stg2[gi][:, mc * 512:(mc + 1) * 512], -128.0,
                               p_bc2[:]), [f"stg2_{gi}", "p_bc2"],
                              [f"w2.{gch}"], op0=ALU.add, op1=ALU.mult)
                def load_chn(chn):
                    ci = chn % 2
                    l2b, l2r = l2c[ci], f"l2c{ci}"
                    P.dma(g, l2b[:].rearrange("p (s c2) -> p s c2", s=tpc),
                          l2d.rearrange("(ch s p) c2 -> ch p s c2", p=128,
                                        s=tpc)[chn],
                          f"l2c{ci}", writes=[l2r])

                def prep_chn(chn):
                    ci = chn % 2
                    l2b, l2r = l2c[ci], f"l2c{ci}"
                    hb, hbr = h_fm[ci], f"h_fm{ci}"
                    for tt2 in range(tpc):
                        src = l2b[:, tt2 * C:(tt2 + 1) * C]
                        tt_g = chn * tpc + tt2
                        P.dve("tensor_scalar",
                              (nrmt[:, 0:C], src,
                               muX[:, tt_g:tt_g + 1], rX[:, tt_g:tt_g + 1]),
                              [l2r, "muX", "rX"],
                              ["nrmt"], op0=ALU.subtract, op1=ALU.mult)
                        for cc in range(8):
                            P.tr(p_h[:, cc * 128:(cc + 1) * 128],
                                 nrmt[:, cc * 128: cc * 128 + 128],
                                 id_bf[:], ["nrmt", "id_bf"],
                                 ["p_h"])
                        for cc in range(8):
                            P.dve("tensor_scalar",
                                  (hb[:, cc * TW + tt2 * 128: cc * TW + tt2 * 128 + 128],
                                   p_h[:, cc * 128:(cc + 1) * 128],
                                   gcol[:, cc:cc + 1], bcol[:, cc:cc + 1]),
                                  ["p_h", "gcol", "bcol"],
                                  [f"{hbr}.{cc}.{tt2}"],
                                  op0=ALU.mult, op1=ALU.add)

                nchn = n_tiles // tpc
                load_chn(0)
                prep_chn(0)
                for chn in range(nchn):
                    ci = chn % 2
                    if chn + 1 < nchn:
                        load_chn(chn + 1)
                    l2b, l2r = l2c[ci], f"l2c{ci}"
                    hb, hbr = h_fm[ci], f"h_fm{ci}"
                    for ft in range(32):
                        py, pyr = p_y[ft % 2], f"p_y{ft % 2}"
                        for cc in range(8):
                            P.mm(py[:],
                                 w1[:, cc * FF + ft * 128: cc * FF + ft * 128 + 128],
                                 hb[:, cc * TW:(cc + 1) * TW],
                                 [f"w1.{cc}", f"{hbr}.{cc}.0",
                                  f"{hbr}.{cc}.{tpc - 1}"],
                                 [pyr], start=(cc == 0), stop=(cc == 7))
                        P.act(y_fm[:, ft * TW:(ft + 1) * TW], py[:], AF.Silu,
                              [pyr, "b1col"], [f"y_fm.{ft}"],
                              bias=b1col[:, ft:ft + 1])
                    if chn + 1 < nchn:
                        prep_chn(chn + 1)
                    for tt2 in range(tpc):
                        tt = chn * tpc + tt2
                        tsl = slice(tt * 128, tt * 128 + 128)
                        pz, pzr = p_z[tt2 % 2], f"p_z{tt2 % 2}"
                        fr, frr = frt[tt2], f"frt{tt2}"
                        for mc in range(2):
                            for ft in range(32):
                                P.mm(pz[:, mc * 512:(mc + 1) * 512],
                                     y_fm[:, ft * TW + tt2 * 128: ft * TW + tt2 * 128 + 128],
                                     w2[:, ft * C + mc * 512: ft * C + mc * 512 + 512],
                                     [f"y_fm.{ft}", f"w2.{ft}"], [pzr],
                                     start=(ft == 0), stop=(ft == 31))
                        P.dve("tensor_tensor",
                              (fr[:], pz[:], l2b[:, tt2 * C:(tt2 + 1) * C]),
                              [pzr, l2r], [frr], op=ALU.add)
                        P.dve("tensor_tensor", (fr[:], fr[:], b2hbc[:]),
                              [frr, "b2hbc"], [frr], op=ALU.add)
                        stats_ops(2 + tt2, fr[:], [frr], 1)
                        P.dma("scalar", x2c[tt2][:], x2d[tsl], "x2c0",
                              writes=["x2c0"])
                        P.dve("tensor_scalar", (x2c[tt2][:], x2c[tt2][:],
                                                st_nm[2 + tt2][:, 1:2], None),
                              ["x2c0", f"st_nm{2 + tt2}"], ["x2c0"],
                              op0=ALU.add)
                        P.dve("scalar_tensor_tensor",
                              (fr[:], fr[:], st_r[2 + tt2][:, 1:2],
                               x2c[tt2][:]),
                              [frr, f"st_r{2 + tt2}", "x2c0"], [frr],
                              op0=ALU.mult, op1=ALU.add)
                        P.dma(sy, yout[tsl], fr[:], "y_w", reads=[frr])
            P.end_block()

    return builder




def make_attn_qs_builder(Tb, drams):
    nch = Tb // 512
    nj = nch // 4
    ntok = nj * 512
    ntiles = Tb // 128
    EX = [16 * (j + 1) for j in range(nj)]
    EOFF = [sum(EX[:j]) for j in range(nj)]
    ETOT = sum(EX)
    (xin, wqin, wkin, wvin, dirsin, hsin, idin, m0in, qposin, jmin,
     og_i, k_dram, v_dram, xoin) = drams

    def builder(P, nc):
        es = ExitStack()
        with es:
            sb = lambda n, s, d: es.enter_context(
                nc.sbuf_tensor(n, s, d, allow_name_mangling=True))
            g, sy = "gpsimd", "sync"

            wq_s = sb("wq_s", [128, 8 * 1024], BF16)
            wk_s = sb("wk_s", [128, 8 * 1024], BF16)
            wv_s = sb("wv_s", [128, 8 * 1024], BF16)
            id_bf = sb("id_bf", [128, 128], BF16)
            id_sc = sb("id_sc", [128, 128], BF16)
            m0 = sb("m0s", [128, 128], BF16)
            bpal = sb("bpal", [128, 8 * 128], BF16)
            dva = sb("dva", [1, 128], BF16)
            dvb = sb("dvb", [1, 128], BF16)
            dvsa = sb("dvsa", [1, 128], BF16)
            dvsb = sb("dvsb", [1, 128], BF16)
            dva_f = sb("dva_f", [1, 128], F32)
            dvb_f = sb("dvb_f", [1, 128], F32)
            dvs_f = sb("dvs_f", [1, 128], F32)
            hs_s = sb("hs_s", [1, 16], F32)
            onesc = sb("onesc", [128, 1], BF16)
            ones_f = sb("ones_f", [1, 128], F32)
            epst = sb("epst", [128, 1], F32)
            jm = sb("jm", [128, ETOT], F32)
            qp_s = sb("qp_s", [1, nj], I32)
            Qp = sb("Qp", [128, 8 * ntok], BF16)

            with ExitStack() as es0:
                sb0 = lambda n, s, d: es0.enter_context(
                    nc.sbuf_tensor(n, s, d, allow_name_mangling=True))
                ps0 = lambda n, s, d: es0.enter_context(
                    nc.psum_tensor(n, s, d, allow_name_mangling=True))
                x_t = [sb0(f"x_t{i}", [128, 4 * C], BF16) for i in range(3)]
                ln_t = [sb0(f"ln_t{i}", [128, 4 * C], BF16) for i in range(2)]
                ln_f = [sb0(f"ln_f{i}", [128, 8 * 512], BF16) for i in range(2)]
                sq_scr = [sb0(f"sq_scr{i}", [128, C], BF16) for i in range(2)]
                cp_scr = [sb0(f"cp_scr{i}", [128, C], BF16) for i in range(2)]
                st_sx = [sb0(f"st_sx{i}", [128, 4], F32) for i in range(2)]
                st_sq = [sb0(f"st_sq{i}", [128, 4], F32) for i in range(2)]
                st_mu = [sb0(f"st_mu{i}", [128, 4], F32) for i in range(2)]
                st_v1 = [sb0(f"st_v1{i}", [128, 4], F32) for i in range(2)]
                st_v2 = [sb0(f"st_v2{i}", [128, 4], F32) for i in range(2)]
                st_sd = [sb0(f"st_sd{i}", [128, 4], F32) for i in range(2)]
                st_r = [sb0(f"st_r{i}", [128, 4], F32) for i in range(2)]
                qtmp = [sb0(f"qtmp{i}", [128, 512], BF16)
                        for i in range(2)]
                kstg = [sb0(f"kstg{i}", [128, 512], BF16) for i in range(8)]
                vstg = [sb0(f"vstg{i}", [128, 4 * 8 * 130], BF16)
                        for i in range(2)]
                p_k = [ps0(f"p_k{i}", [128, 512], F32) for i in range(2)]
                p_v = [ps0(f"p_v{i}", [128, 1024], F32) for i in range(2)]
                p_t = [ps0(f"p_t{i}", [128, 512], BF16) for i in range(2)]

                def ln_chunk(ci, xb, xr, lnb, lnr):
                    sqs, sqr = sq_scr[ci], f"sq_scr{ci}"
                    sx, sxr = st_sx[ci], f"st_sx{ci}"
                    sq, sqr2 = st_sq[ci], f"st_sq{ci}"
                    mu, mur = st_mu[ci], f"st_mu{ci}"
                    v1, v1r = st_v1[ci], f"st_v1{ci}"
                    v2, v2r = st_v2[ci], f"st_v2{ci}"
                    sd, sdr = st_sd[ci], f"st_sd{ci}"
                    sr, srr = st_r[ci], f"st_r{ci}"
                    for sub in range(4):
                        P.act(sqs[:], xb[:, sub * C:(sub + 1) * C], AF.Square,
                              [xr], [sqr, sqr2], accum=sq[:, sub:sub + 1])
                    P.dve("tensor_reduce",
                          (sx[:].rearrange("p (s o) -> p s o", o=1),
                           xb[:].rearrange("p (s c) -> p s c", s=4)),
                          [xr], [sxr], axis=mybir.AxisListType.X, op=ALU.add)
                    P.dve("tensor_scalar", (mu[:], sx[:], 1.0 / C, None),
                          [sxr], [mur], op0=ALU.mult)
                    P.dve("tensor_tensor", (v1[:], mu[:], mu[:]), [mur], [v1r],
                          op=ALU.mult)
                    P.dve("scalar_tensor_tensor", (v2[:], sq[:], 1.0 / C,
                                                   v1[:]), [sqr2, v1r], [v2r],
                          op0=ALU.mult, op1=ALU.subtract)
                    P.act(sd[:], v2[:], AF.Sqrt, [v2r, "epst"], [sdr],
                          bias=epst[:, 0:1])
                    P.dve("reciprocal", (sr[:], sd[:]), [sdr], [srr])
                    for sub in range(4):
                        P.dve("tensor_scalar",
                              (lnb[:, sub * C:(sub + 1) * C],
                               xb[:, sub * C:(sub + 1) * C],
                               mu[:, sub:sub + 1], sr[:, sub:sub + 1]),
                              [xr, mur, srr], [lnr],
                              op0=ALU.subtract, op1=ALU.mult)

                P.dma(g, id_bf[:], idin, "c_id", writes=["id_bf"])
                P.dma(g, m0[:], m0in, "c_m0", writes=["m0"])
                P.dma(sy, hs_s[:], hsin, "c_hs", writes=["hs_s"])
                P.dma(sy, jm[:], jmin, "c_jm", writes=["jm"])
                P.dma(sy, qp_s[:], qposin, "c_qp", writes=["qp_s"])
                P.dma(g, wq_s[:].rearrange("p (cc d) -> p cc d", cc=8),
                      wqin.rearrange("(cc p) d -> p cc d", p=128), "c_wq",
                      writes=["wq_s"])
                P.dma(g, wk_s[:].rearrange("p (cc d) -> p cc d", cc=8),
                      wkin.rearrange("(cc p) d -> p cc d", p=128), "c_wk",
                      writes=["wk_s"])
                P.dma(g, wv_s[:].rearrange("p (cc d) -> p cc d", cc=8),
                      wvin.rearrange("(cc p) d -> p cc d", p=128), "c_wv",
                      writes=["wv_s"])
                P.dve("memset", (onesc[:], 1.0), [], ["onesc"])
                P.dve("memset", (ones_f[:], 1.0), [], ["ones_f"])
                P.dve("memset", (epst[:], EPS), [], ["epst"])
                P.act(id_sc[:], id_bf[:], AF.Copy, ["id_bf"], ["id_sc"],
                      scale=SCALE)
                def ln_stage(ch):
                    xb, xr = x_t[ch % 3], f"x_t{ch % 3}"
                    lnb, lnr = ln_t[ch % 2], f"ln_t{ch % 2}"
                    P.dma(g, xb[:].rearrange("p (s c) -> p s c", s=4),
                          xin.rearrange("(ch s p) c -> ch p s c", p=128, s=4)[ch],
                          f"xt{ch % 3}", writes=[xr])
                    ln_chunk(ch % 2, xb, xr, lnb, lnr)

                ln_stage(0)
                for ch in range(nch):
                    lnb, lnr = ln_t[ch % 2], f"ln_t{ch % 2}"
                    lfb, lfr = ln_f[ch % 2], f"ln_f{ch % 2}"
                    if ch + 1 < nch:
                        ln_stage(ch + 1)
                    for cc in range(8):
                        pt, ptr_ = p_t[cc % 2], f"p_t{cc % 2}"
                        for sub in range(4):
                            P.tr(pt[:, sub * 128:(sub + 1) * 128],
                                 lnb[:, sub * C + cc * 128: sub * C + cc * 128 + 128],
                                 id_bf[:], [lnr, "id_bf"], [ptr_])
                        P.dve("tensor_copy",
                              (lfb[:, cc * 512:(cc + 1) * 512], pt[:]),
                              [ptr_], [f"{lfr}.{cc}"])
                    for p in range(8):
                        pk, pkr = p_k[p % 2], f"p_k{p % 2}"
                        for cc in range(8):
                            P.mm(pk[:],
                                 wk_s[:, cc * 1024 + p * 128: cc * 1024 + p * 128 + 128],
                                 lfb[:, cc * 512:(cc + 1) * 512],
                                 ["wk_s", f"{lfr}.{cc}"], [pkr],
                                 start=(cc == 0), stop=(cc == 7))
                        ks, ksr = kstg[p], f"kstg{p}"
                        if p % 2 == 0:
                            P.act(ks[:], pk[:], AF.Copy, [pkr], [ksr])
                        else:
                            P.dve("tensor_copy", (ks[:], pk[:]), [pkr], [ksr])
                        P.dma(sy, k_dram[p][:, ch * 512:(ch + 1) * 512],
                              ks[:], ksr, reads=[ksr], writes=["kspill"])
                    vs, vsr = vstg[ch % 2], f"vstg{ch % 2}"
                    vv = vs[:].rearrange("q (pp kt t) -> q pp kt t", pp=8,
                                         kt=4)
                    P.dve("memset",
                          (vv[:, :, :, 64:130:65].rearrange("q pp kt u -> q pp kt u"), 1.0),
                          [], [vsr])
                    for kt in range(4):
                        pv, pvr = p_v[kt % 2], f"p_v{kt % 2}"
                        for cc in range(8):
                            for mh in range(2):
                                P.mm(pv[:, mh * 512:(mh + 1) * 512],
                                     lfb[:, cc * 512 + kt * 128: cc * 512 + kt * 128 + 128],
                                     wv_s[:, cc * 1024 + mh * 512: cc * 1024 + mh * 512 + 512],
                                     [f"{lfr}.{cc}", "wv_s"], [pvr],
                                     start=(cc == 0), stop=(cc == 7))
                        src = pv[:].rearrange("q (pp two d) -> q pp two d",
                                              pp=8, two=2)
                        dst = vv[:, :, kt, :].rearrange(
                            "q pp (two u) -> q pp two u",
                            two=2)[:, :, :, 0:64]
                        P.dve("tensor_copy", (dst, src), [pvr], [vsr])
                    for p in range(8):
                        P.dma(sy,
                              v_dram[p][:, ch * 520:(ch + 1) * 520],
                              vs[:, p * 520:(p + 1) * 520],
                              vsr, reads=[vsr], writes=["vspill"])

                for p in range(8):
                    P.dve("memset", (dva_f[:], 0.0), [], ["dva_f"])
                    P.dve("memset", (dvb_f[:], 0.0), [], ["dvb_f"])
                    P.dma(g, dva_f[0:1, 0:8], dirsin[2 * p:2 * p + 1, :],
                          "c_dirs", reads=["dva_f"], writes=["dva_f"])
                    P.dma(g, dvb_f[0:1, 64:72], dirsin[2 * p + 1:2 * p + 2, :],
                          "c_dirs2", reads=["dvb_f"], writes=["dvb_f"])
                    P.dve("tensor_scalar", (dvs_f[0:1, :], dva_f[0:1, :],
                                            hs_s[0:1, 2 * p:2 * p + 1], None),
                          ["dva_f", "hs_s"], ["dvs_f"], op0=ALU.mult)
                    P.dve("tensor_copy", (dvsa[:], dvs_f[:]), ["dvs_f"],
                          ["dvsa"])
                    P.dve("tensor_scalar", (dvs_f[0:1, :], dvb_f[0:1, :],
                                            hs_s[0:1, 2 * p + 1:2 * p + 2],
                                            None),
                          ["dvb_f", "hs_s"], ["dvs_f"], op0=ALU.mult)
                    P.dve("tensor_copy", (dvsb[:], dvs_f[:]), ["dvs_f"],
                          ["dvsb"])
                    P.dve("tensor_copy", (dva[:], dva_f[:]), ["dva_f"],
                          ["dva"])
                    P.dve("tensor_copy", (dvb[:], dvb_f[:]), ["dvb_f"],
                          ["dvb"])
                    P.mm(p_k[0][0:128, 0:128], dva[0:1, :], dvsa[0:1, :],
                         ["dva", "dvsa"], ["p_k0"], start=True, stop=False)
                    P.mm(p_k[0][0:128, 0:128], dvb[0:1, :], dvsb[0:1, :],
                         ["dvb", "dvsb"], ["p_k0"], start=False, stop=False)
                    P.mm(p_k[0][0:128, 0:128], id_bf[:], id_sc[:],
                         ["id_bf", "id_sc"], ["p_k0"], start=False, stop=True)
                    P.dve("tensor_copy", (bpal[:, p * 128:(p + 1) * 128],
                                          p_k[0][0:128, 0:128]), ["p_k0"],
                          ["bpal"])

                def qln_stage(u):
                    xb, xr = x_t[u % 3], f"x_t{u % 3}"
                    lnb, lnr = ln_t[u % 2], f"ln_t{u % 2}"
                    P.dma(g, xb[:].rearrange("p (s c) -> p s c", s=4),
                          xoin.rearrange("(u2 s p) c -> u2 p s c", p=128,
                                         s=4)[u],
                          f"xq{u % 3}", writes=[xr])
                    ln_chunk(u % 2, xb, xr, lnb, lnr)

                qln_stage(0)
                for u in range(nj):
                    lnb, lnr = ln_t[u % 2], f"ln_t{u % 2}"
                    lfb, lfr = ln_f[u % 2], f"ln_f{u % 2}"
                    if u + 1 < nj:
                        qln_stage(u + 1)
                    for cc in range(8):
                        pt, ptr_ = p_t[cc % 2], f"p_t{cc % 2}"
                        for sub in range(4):
                            P.tr(pt[:, sub * 128:(sub + 1) * 128],
                                 lnb[:, sub * C + cc * 128: sub * C + cc * 128 + 128],
                                 id_bf[:], [lnr, "id_bf"], [ptr_])
                        P.dve("tensor_copy",
                              (lfb[:, cc * 512:(cc + 1) * 512], pt[:]),
                              [ptr_], [f"{lfr}.{cc}"])
                    for p in range(8):
                        a = p % 2
                        for cc in range(8):
                            P.mm(p_k[a][:],
                                 wq_s[:, cc * 1024 + p * 128: cc * 1024 + p * 128 + 128],
                                 lfb[:, cc * 512:(cc + 1) * 512],
                                 ["wq_s", f"{lfr}.{cc}"], [f"p_k{a}"],
                                 start=(cc == 0), stop=(cc == 7))
                        P.act(qtmp[a][:], p_k[a][:], AF.Copy, [f"p_k{a}"],
                              [f"qtmp{a}"])
                        P.mm(p_v[a][:, 0:512], bpal[:, p * 128:(p + 1) * 128],
                             qtmp[a][:], ["bpal", f"qtmp{a}"], [f"p_v{a}"])
                        P.act(Qp[:, p * ntok + u * 512: p * ntok + u * 512 + 512],
                              p_v[a][:, 0:512], AF.Copy, [f"p_v{a}"], ["Qp"])
            P.end_block()

            with ExitStack() as es1:
                sb1 = lambda n, s, d: es1.enter_context(
                    nc.sbuf_tensor(n, s, d, allow_name_mangling=True))
                ps1 = lambda n, s, d: es1.enter_context(
                    nc.psum_tensor(n, s, d, allow_name_mangling=True))
                Kf = [sb1(f"Kf{i}", [128, Tb], BF16) for i in range(2)]
                Vf = [sb1(f"Vf{i}", [128, ntiles * 130], BF16)
                      for i in range(2)]
                kdg = [sb1(f"kdg{i}", [128, 4 * 512], BF16) for i in range(nj)]
                vdg = [sb1(f"vdg{i}", [128, 4 * 520], BF16) for i in range(nj)]
                pa_sb = [sb1(f"pa_sb{i}", [128, 512], BF16) for i in range(3)]
                pb_sb = [sb1(f"pb_sb{i}", [128, 512], BF16) for i in range(3)]
                oev = sb1("oev", [128, 512], F32)
                obuf = [sb1(f"obuf{i}", [128, 512], BF16) for i in range(4)]
                nsq = sb1("nsq", [128, 4096], BF16)
                nrow = sb1("nrow", [1, 512], F32)
                nmax = sb1("nmax", [1, 64], F32)
                ntot = sb1("ntot", [1, 8], F32)
                ebias = sb1("ebias", [128, 8], F32)
                jb = [sb1(f"jb{i}", [128, ETOT], F32) for i in range(2)]
                pa = [ps1(f"pa{i}", [128, 512], F32) for i in range(2)]
                pb = [ps1(f"pb{i}", [128, 512], F32) for i in range(2)]
                oa = ps1("oa", [128, 512], F32)
                ob = ps1("ob", [128, 512], F32)
                pn = ps1("p_n", [128, 512], F32)

                fcells = {u: [] for u in range(nj)}
                for u in range(nj):
                    def _ldf(e, cell=fcells[u], uu=u):
                        cell.append(e.value_load(qp_s[0:1, uu:uu + 1]))
                        return e.nop()
                    P.op("gpsimd", _ldf, ["qp_s"], [f"fposv{u}"])

                def emit_kv_load(p2):
                    kf_, kfr_ = Kf[p2 % 2], f"Kf{p2 % 2}"
                    vf_, vfr_ = Vf[p2 % 2], f"Vf{p2 % 2}"
                    P.dma(g, kf_[:], k_dram[p2], kfr_, reads=["kspill"],
                          writes=[kfr_])
                    P.dma(g, vf_[:], v_dram[p2], vfr_, reads=["vspill"],
                          writes=[vfr_])

                def norm_steps(p2):
                    kf_, kfr_ = Kf[p2 % 2], f"Kf{p2 % 2}"
                    steps = []
                    for which, buf, br, wid in ((0, kf_, kfr_, Tb),
                                                (1, Qp, "Qp", ntok)):
                        nseg2 = wid // 2048 if wid >= 2048 else 1
                        seglen = min(2048, wid)
                        for seg in range(nseg2):
                            s0 = ((p2 * ntok if which == 1 else 0)
                                  + seg * seglen)
                            sq4 = seg % 2

                            def _sq(buf=buf, br=br, s0=s0, seglen=seglen,
                                    sq4=sq4):
                                P.dve("tensor_tensor",
                                      (nsq[:, sq4 * 2048: sq4 * 2048 + seglen],
                                       buf[:, s0:s0 + seglen],
                                       buf[:, s0:s0 + seglen]), [br],
                                      [f"nsq{sq4}"], op=ALU.mult)
                            steps.append(_sq)
                            for j2 in range(seglen // 512):
                                def _mmr(seg=seg, j2=j2, sq4=sq4):
                                    r2 = 32 * (j2 % 2)
                                    P.mm(pn[r2:r2 + 1, 0:512], onesc[:, 0:1],
                                         nsq[:, sq4 * 2048 + j2 * 512:
                                             sq4 * 2048 + (j2 + 1) * 512],
                                         ["onesc", f"nsq{sq4}"],
                                         [f"p_n.{j2 % 2}"])
                                    P.dve("tensor_reduce",
                                          (nmax[0:1, seg * 4 + j2:
                                                seg * 4 + j2 + 1],
                                           pn[r2:r2 + 1, 0:512]),
                                          [f"p_n.{j2 % 2}"], ["nmax"],
                                          axis=mybir.AxisListType.X,
                                          op=ALU.max)
                                steps.append(_mmr)

                        def _fin(which=which, nseg2=nseg2, seglen=seglen):
                            P.dve("tensor_reduce",
                                  (ntot[0:1, which:which + 1],
                                   nmax[0:1, 0:nseg2 * (seglen // 512)]),
                                  ["nmax"], ["ntot"],
                                  axis=mybir.AxisListType.X, op=ALU.max)
                        steps.append(_fin)

                    def _mul():
                        P.dve("tensor_tensor", (nrow[0:1, 0:1],
                                                ntot[0:1, 0:1],
                                                ntot[0:1, 1:2]), ["ntot"],
                              ["nrow"], op=ALU.mult)
                    steps.append(_mul)
                    return steps

                def emit_norms(p2):
                    for st in norm_steps(p2):
                        st()

                def emit_ebias(p2):
                    jbp, jbr = jb[p2 % 2], f"jb{p2 % 2}"
                    P.act(nrow[0:1, 3:4], nrow[0:1, 0:1], AF.Ln, ["nrow"],
                          ["nrow"])
                    P.act(nrow[0:1, 1:2], nrow[0:1, 3:4], AF.Exp, ["nrow"],
                          ["nrow"], scale=0.5)
                    P.dve("tensor_scalar", (nrow[0:1, 2:3], nrow[0:1, 1:2],
                                            -1.0, None), ["nrow"], ["nrow"],
                          op0=ALU.mult)
                    P.mm(pn[:, 0:1], ones_f[:], nrow[0:1, 2:3],
                         ["ones_f", "nrow"], ["p_n"])
                    P.dve("tensor_copy", (ebias[:, p2:p2 + 1], pn[:, 0:1]),
                          ["p_n"], [f"ebias.{p2}"])
                    P.dve("tensor_scalar", (jbp[:], jm[:],
                                            ebias[:, p2:p2 + 1], None),
                          ["jm", f"ebias.{p2}"], [jbr], op0=ALU.add)

                pend_norm = []
                emit_kv_load(0)
                emit_norms(0)
                emit_ebias(0)
                for p in range(8):
                    if p % 4 == 0:
                        hh = p // 4
                        for u in range(nj):
                            def _dk(e, cell=fcells[u], dst=kdg[u], h2=hh):
                                return e.dma_start(
                                    dst[:].rearrange("q (pp t) -> q pp t",
                                                     pp=4),
                                    k_dram[4 * h2:4 * h2 + 4, :,
                                           bass.ts(cell[0], 512)].rearrange(
                                        "pp q t -> q pp t"))
                            P.op("gpsimd", _dk, [f"fposv{u}", "kspill"],
                                 [f"kdg{u}"], dma=True, dkey=f"kdgd{u}")

                            def _dv(e, cell=fcells[u], dst=vdg[u], h2=hh):
                                return e.dma_start(
                                    dst[:].rearrange("q (pp t) -> q pp t",
                                                     pp=4),
                                    v_dram[4 * h2:4 * h2 + 4, :,
                                           bass.ts(cell[0], 520)].rearrange(
                                        "pp q t -> q pp t"))
                            P.op("gpsimd", _dv, [f"fposv{u}", "vspill"],
                                 [f"vdg{u}"], dma=True, dkey=f"vdgd{u}")
                    kf, kfr = Kf[p % 2], f"Kf{p % 2}"
                    vf, vfr = Vf[p % 2], f"Vf{p % 2}"

                    for u in range(nj):
                        E = EX[u]
                        off = EOFF[u]
                        kd, kdr = kdg[u], f"kdg{u}"
                        vd, vdr = vdg[u], f"vdg{u}"
                        pq = p % 4

                        qsl = slice(p * ntok + u * 512, p * ntok + u * 512 + 512)

                        def emit_s(kt):
                            if kt >= E - 4:
                                d = kt - (E - 4)
                                ka = kd[0:64, pq * 512 + d * 128: pq * 512 + (d + 1) * 128]
                                kb = kd[64:128, pq * 512 + d * 128: pq * 512 + (d + 1) * 128]
                                kres = [kdr, "Qp"]
                            else:
                                ka = kf[0:64, kt * 128:(kt + 1) * 128]
                                kb = kf[64:128, kt * 128:(kt + 1) * 128]
                                kres = [kfr, "Qp"]
                            P.mm(pa[kt % 2][:], ka, Qp[0:64, qsl], kres,
                                 [f"pa{kt % 2}"], skip=True)
                            P.mm(pb[kt % 2][:], kb, Qp[64:128, qsl], kres,
                                 [f"pb{kt % 2}"], skip=True)

                        def emit_exp(kt):
                            diag = kt >= E - 4
                            for hd_, psum, pres in ((0, pa[kt % 2], f"pa{kt % 2}"),
                                                    (1, pb[kt % 2], f"pb{kt % 2}")):
                                psb = (pa_sb, pb_sb)[hd_][kt % 3]
                                pr_ = f"{('pa_sb', 'pb_sb')[hd_]}{kt % 3}"
                                if not diag:
                                    P.act(psb[:], psum[:], AF.Exp,
                                          [pres, f"jb{p % 2}"], [pr_],
                                          bias=jb[p % 2][:, off + kt: off + kt + 1])
                                else:
                                    dd = (kt - (E - 4)) * 128
                                    if dd > 0:
                                        P.dve("memset", (psb[:, 0:dd], 0.0),
                                              [], [pr_])
                                    P.act(psb[:, dd:512], psum[:, dd:512],
                                          AF.Exp, [pres, f"ebias.{p}"], [pr_],
                                          bias=ebias[:, p:p + 1])
                                    P.dve("tensor_tensor",
                                          (psb[:, dd:dd + 128],
                                           psb[:, dd:dd + 128], m0[:]),
                                          [pr_, "m0s"], [pr_], op=ALU.mult)

                        def emit_av(kt):
                            if kt >= E - 4:
                                d = kt - (E - 4)
                                va = vd[:, pq * 520 + d * 130: pq * 520 + d * 130 + 65]
                                vb2 = vd[:, pq * 520 + d * 130 + 65: pq * 520 + d * 130 + 130]
                                vres = vdr
                            else:
                                va = vf[:, kt * 130: kt * 130 + 65]
                                vb2 = vf[:, kt * 130 + 65: kt * 130 + 130]
                                vres = vfr
                            psa = pa_sb[kt % 3]
                            psar = f"pa_sb{kt % 3}"
                            psbb = pb_sb[kt % 3]
                            psbr = f"pb_sb{kt % 3}"
                            st_ = kt == 0
                            sp_ = kt == E - 1
                            P.mm(oa[0:65, :], va, psa[:],
                                 [vres, psar], ["oa"], start=st_,
                                 stop=sp_, skip=True)
                            P.mm(ob[0:65, :], vb2, psbb[:],
                                 [vres, psbr], ["ob"], start=st_,
                                 stop=sp_, skip=True)

                        for kt in range(E + 2):
                            if kt < E:
                                emit_s(kt)
                            if 1 <= kt <= E:
                                emit_exp(kt - 1)
                            if kt >= 2:
                                emit_av(kt - 2)
                            if u == 2 and p < 7 and kt % 2 == 0 and pend_norm:
                                pend_norm.pop(0)()
                        for hd_, opair, oname in ((0, oa, "oa"),
                                                  (1, ob, "ob")):
                            ob_ = obuf[hd_ * 2 + u % 2]
                            obr = f"obuf{hd_ * 2 + u % 2}"
                            P.dve("tensor_copy",
                                  (ob_[0:65, :], opair[0:65, :]),
                                  [oname], [obr])
                            P.dma(sy,
                                  og_i[p, hd_ * 65: hd_ * 65 + 65,
                                       u * 512:(u + 1) * 512],
                                  ob_[0:65, :], obr, reads=[obr, "og_w"],
                                  writes=["og_w"])
                        if u == 0 and p < 7:
                            emit_kv_load(p + 1)
                        if u == 1 and p < 7:
                            pend_norm = norm_steps(p + 1)
                        if u == 2 and p < 7:
                            while pend_norm:
                                pend_norm.pop(0)()
                        if u == 3 and p < 7:
                            emit_ebias(p + 1)
            P.end_block()

    return builder


def build_fused(Tb):
    NT = NB * Tb
    ntok = NT // NCORES
    nj = ntok // 512
    nch = Tb // 512
    ETOT = sum(16 * (j + 1) for j in range(nj))
    ntiles = Tb // 128
    nc = bass.Bass("TRN2", target_bir_lowering=False)
    og_i = nc.dram_tensor("og_i", [NCORES, 130, ntok], BF16).ap()
    k_dram = nc.dram_tensor("k_dram", [8, 128, Tb], BF16).ap()
    v_dram = nc.dram_tensor("v_dram", [8, 128, ntiles * 130], BF16).ap()
    attn_drams = (
        nc.dram_tensor("xin", [Tb, C], F32, kind="ExternalInput").ap(),
        nc.dram_tensor("wq", [C, C], F32, kind="ExternalInput").ap(),
        nc.dram_tensor("wk", [C, C], F32, kind="ExternalInput").ap(),
        nc.dram_tensor("wv", [C, C], F32, kind="ExternalInput").ap(),
        nc.dram_tensor("dirs", [16, 8], F32, kind="ExternalInput").ap(),
        nc.dram_tensor("hs", [1, 16], F32, kind="ExternalInput").ap(),
        nc.dram_tensor("ident", [128, 128], F32, kind="ExternalInput").ap(),
        nc.dram_tensor("m0", [128, 128], F32, kind="ExternalInput").ap(),
        nc.dram_tensor("qpos", [1, nj], I32, kind="ExternalInput").ap(),
        nc.dram_tensor("jm", [128, ETOT], F32, kind="ExternalInput").ap(),
        og_i, k_dram, v_dram, None,
    )
    ffn_drams = (
        None,
        nc.dram_tensor("xo", [ntok, C], F32, kind="ExternalInput").ap(),
        nc.dram_tensor("wout", [C, C], F32, kind="ExternalInput").ap(),
        nc.dram_tensor("q1", [C, FF], I32, kind="ExternalInput").ap(),
        nc.dram_tensor("s1", [8, FF], F32, kind="ExternalInput").ap(),
        nc.dram_tensor("b1", [1, FF], F32, kind="ExternalInput").ap(),
        nc.dram_tensor("q2", [FF, C], I32, kind="ExternalInput").ap(),
        nc.dram_tensor("s2", [32, C], F32, kind="ExternalInput").ap(),
        nc.dram_tensor("b2", [1, C], F32, kind="ExternalInput").ap(),
        nc.dram_tensor("gm", [1, C], F32, kind="ExternalInput").ap(),
        nc.dram_tensor("bt", [1, C], F32, kind="ExternalInput").ap(),
        nc.dram_tensor("ident2", [128, 128], F32, kind="ExternalInput").ap(),
        nc.dram_tensor("y", [ntok, C], F32, kind="ExternalOutput").ap(),
        nc.dram_tensor("x2d", [ntok, C], F32, kind="Internal").ap(),
        nc.dram_tensor("l2d", [ntok, C], BF16, kind="Internal").ap(),
    )

    def og_rows(r, a, b2_):
        return og_i[r, a:b2_, :]

    attn_drams = attn_drams[:-1] + (ffn_drams[1],)
    attn_b = make_attn_qs_builder(Tb, attn_drams)
    ffn_b = make_ffn_builder(ntok, ffn_drams, og_rows=og_rows, coll=None)

    def fused(P, nc_):
        attn_b(P, nc_)
        ffn_b(P, nc_)

    return two_pass(nc, fused)


def core_token_slots(Tb):
    nj = (NB * Tb // NCORES) // 512
    out = {}
    for c in range(4):
        out[c] = [(4 * j + c) if j % 2 == 0 else (4 * j + 3 - c)
                  for j in range(nj)]
    return out


def build_ffn(ntok):
    nc = bass.Bass("TRN2", target_bir_lowering=False)
    drams = (
        nc.dram_tensor("og", [1040, ntok], BF16, kind="ExternalInput").ap(),
        nc.dram_tensor("xo", [ntok, C], F32, kind="ExternalInput").ap(),
        nc.dram_tensor("wout", [C, C], F32, kind="ExternalInput").ap(),
        nc.dram_tensor("q1", [C, FF], I32, kind="ExternalInput").ap(),
        nc.dram_tensor("s1", [8, FF], F32, kind="ExternalInput").ap(),
        nc.dram_tensor("b1", [1, FF], F32, kind="ExternalInput").ap(),
        nc.dram_tensor("q2", [FF, C], I32, kind="ExternalInput").ap(),
        nc.dram_tensor("s2", [32, C], F32, kind="ExternalInput").ap(),
        nc.dram_tensor("b2", [1, C], F32, kind="ExternalInput").ap(),
        nc.dram_tensor("gm", [1, C], F32, kind="ExternalInput").ap(),
        nc.dram_tensor("bt", [1, C], F32, kind="ExternalInput").ap(),
        nc.dram_tensor("ident", [128, 128], F32, kind="ExternalInput").ap(),
        nc.dram_tensor("y", [ntok, C], F32, kind="ExternalOutput").ap(),
        nc.dram_tensor("x2d", [ntok, C], F32, kind="Internal").ap(),
        nc.dram_tensor("l2d", [ntok, C], BF16, kind="Internal").ap(),
    )
    return two_pass(nc, make_ffn_builder(ntok, drams))


T_OVERRIDE = []


def kernel(L0_attn_x, L0_attn_w_qkv, L0_attn_w_out, L0_attn_head_scales,
           L0_attn_head_dirs, L0_ffn_ln_gamma, L0_ffn_ln_beta, L0_ffn_fc1_q,
           L0_ffn_fc1_scales, L0_ffn_fc1_bias, L0_ffn_fc2_q, L0_ffn_fc2_scales,
           L0_ffn_fc2_bias):
    Tb = T_OVERRIDE[0] if T_OVERRIDE else 8192
    NT = NB * Tb
    ntok = NT // NCORES
    nj = ntok // 512
    f32 = np.float32
    x = np.ascontiguousarray(np.asarray(L0_attn_x), dtype=f32).reshape(NT, C)
    wqkv = np.ascontiguousarray(np.asarray(L0_attn_w_qkv), dtype=f32)
    wout = np.ascontiguousarray(np.asarray(L0_attn_w_out), dtype=f32)
    hsc = np.ascontiguousarray(np.asarray(L0_attn_head_scales), dtype=f32)
    hdr = np.ascontiguousarray(np.asarray(L0_attn_head_dirs), dtype=f32)
    gm = np.ascontiguousarray(np.asarray(L0_ffn_ln_gamma), dtype=f32).reshape(1, C)
    bt = np.ascontiguousarray(np.asarray(L0_ffn_ln_beta), dtype=f32).reshape(1, C)
    q1 = np.ascontiguousarray(np.asarray(L0_ffn_fc1_q), dtype=np.int32)
    s1 = np.ascontiguousarray(np.asarray(L0_ffn_fc1_scales), dtype=f32)
    b1 = np.ascontiguousarray(np.asarray(L0_ffn_fc1_bias), dtype=f32).reshape(1, FF)
    q2 = np.ascontiguousarray(np.asarray(L0_ffn_fc2_q), dtype=np.int32)
    s2 = np.ascontiguousarray(np.asarray(L0_ffn_fc2_scales), dtype=f32)
    b2 = np.ascontiguousarray(np.asarray(L0_ffn_fc2_bias), dtype=f32).reshape(1, C)

    ident = np.eye(128, dtype=f32)
    m0 = (np.arange(128)[None, :] >= np.arange(128)[:, None]).astype(f32)
    slots = core_token_slots(Tb)
    EX = [16 * (j + 1) for j in range(nj)]
    ETOT = sum(EX)

    ncf = build_fused(Tb)
    in_maps = []
    for gg in range(NCORES):
        c = gg % 4
        b = gg // 4
        pos = slots[c]
        own_rows = np.concatenate(
            [np.arange(p * 512, p * 512 + 512) + b * Tb for p in pos])
        jm = np.zeros((128, ETOT), dtype=f32)
        off = 0
        for j in range(nj):
            for s in range(EX[j]):
                if s >= 4 * pos[j] and s < EX[j] - 4:
                    jm[:, off + s] = -1e5
            off += EX[j]
        in_maps.append(dict(
            xin=np.ascontiguousarray(x[b * Tb:(b + 1) * Tb]),
            wq=np.ascontiguousarray(wqkv[:, 0:1024]),
            wk=np.ascontiguousarray(wqkv[:, 1024:2048]),
            wv=np.ascontiguousarray(wqkv[:, 2048:3072]),
            dirs=hdr, hs=hsc.reshape(1, 16),
            ident=ident, m0=m0,
            qpos=np.asarray(pos, dtype=np.int32).reshape(1, nj),
            jm=jm,
            xo=np.ascontiguousarray(x[own_rows]),
            wout=wout, q1=q1, s1=s1, b1=b1, q2=q2, s2=s2, b2=b2,
            gm=gm, bt=bt, ident2=ident,
        ))
    res = run_bass_kernel_spmd(ncf, in_maps, core_ids=list(range(NCORES)))
    y = np.empty((NT, C), dtype=np.float32)
    for gg in range(NCORES):
        c = gg % 4
        b = gg // 4
        own_rows = np.concatenate(
            [np.arange(p * 512, p * 512 + 512) + b * Tb for p in slots[c]])
        y[own_rows] = np.asarray(res.results[gg]["y"], dtype=np.float32)
    return y.reshape(NB, Tb, C)
